# version 16
# baseline (speedup 1.0000x reference)
import numpy as np
import concourse.bass as bass
import concourse.mybir as mybir
from concourse.bass_utils import run_bass_kernel_spmd
from concourse.bass_core_rust import IndirectOffsetOnAxis

F32 = mybir.dt.float32
F16 = mybir.dt.float16
I32 = mybir.dt.int32
AF = mybir.ActivationFunctionType
ALU = mybir.AluOpType

NCORES = 8
D = 1024
TOK = 4096
NT = 32
NG = 8
KC = 8
NF = 8
NXS = 5
EPS = 1e-5

ENGS = ("sync", "tensor", "vector", "scalar", "gpsimd")


def semkey(engine, dma):
    return engine + ("_d" if dma else "_c")


SEMKEYS = ("sync_d", "tensor_c", "vector_c", "vector_d", "scalar_c",
           "scalar_d", "gpsimd_c", "gpsimd_d")


class Sched:
    def __init__(self):
        self.items = []
        self.counts = {k: 0 for k in SEMKEYS}

    def emit(self, engine, fn, reads=(), writes=(), tag="", dma=False,
             manual=False):
        key = semkey(engine, dma)
        inc = 16 if dma else 1
        self.counts[key] += inc
        ev = (key, self.counts[key], inc, manual, engine)
        self.items.append((engine, fn, tuple(reads), tuple(writes), tag, ev))
        return ev


def _ov(a, b):
    return a[0] == b[0] and a[1] < b[2] and b[1] < a[2]


def resolve(sched):
    last_write = {}
    readers = {}
    raw = []
    for engine, fn, reads, writes, tag, ev in sched.items:
        need = []
        for r in reads:
            for b, e in last_write.get(r[0], ()):
                if _ov(r, b):
                    need.append(e)
        for w in writes:
            for b, e in last_write.get(w[0], ()):
                if _ov(w, b):
                    need.append(e)
            for b, e in readers.get(w[0], ()):
                if _ov(w, b):
                    need.append(e)
        per = {}
        for (e_key, e_val, _inc, _m, e_eng) in need:
            per[e_key] = max(per.get(e_key, 0), e_val)
        raw.append(per)
        for w in writes:
            lw = last_write.setdefault(w[0], [])
            lw[:] = [(b, e) for b, e in lw
                     if not (w[1] <= b[1] and b[2] <= w[2])]
            lw.append((w, ev))
            rd = readers.setdefault(w[0], [])
            rd[:] = [(b, e) for b, e in rd
                     if not (w[1] <= b[1] and b[2] <= w[2])]
        for r in reads:
            readers.setdefault(r[0], []).append((r, ev))

    DRAIN_K = 4
    dkey = {e: e + "_d" for e in ("sync", "scalar", "gpsimd", "vector")}
    drains = {}
    cur = {k: 0 for k in SEMKEYS if k.endswith("_d")}
    since = {k: 0 for k in cur}
    bounds = {k: [] for k in cur}
    for idx, ((engine, fn, reads, writes, tag, ev), per) in enumerate(
            zip(sched.items, raw)):
        k = dkey.get(engine)
        if k is not None and (per or (ev[0] == k and since[k] >= DRAIN_K)):
            if since[k] > 0:
                drains.setdefault(idx, {})[k] = cur[k]
                bounds[k].append(cur[k])
                since[k] = 0
        if ev[0] in cur:
            cur[ev[0]] = ev[1]
            since[ev[0]] += 1
    for k in cur:
        if since[k] > 0:
            bounds[k].append(cur[k])
            drains.setdefault(len(sched.items), {})[k] = cur[k]

    import bisect
    def round_up(k, v):
        bl = bounds.get(k)
        if not bl:
            return v
        i = bisect.bisect_left(bl, v)
        return bl[i] if i < len(bl) else bl[-1]

    out = []
    for per in raw:
        rounded = {}
        for k, v in per.items():
            rounded[k] = round_up(k, v) if k.endswith("_d") else v
        out.append(rounded)
    return out, drains


def buf(name, lo=0, hi=1 << 30):
    return (name, lo, hi)


def build_program(detect_races=False, audit=False, dbg=False):
    nc = bass.Bass("TRN2", target_bir_lowering=False,
                   detect_race_conditions=detect_races)

    x_h = nc.dram_tensor("L0_attn_x_shard", [TOK, D], F32, kind="ExternalInput").ap()
    wz_h = nc.dram_tensor("L0_attn_Wz", [D, D], F32, kind="ExternalInput").ap()
    wh_h = nc.dram_tensor("L0_attn_Wh", [D, D], F32, kind="ExternalInput").ap()
    wfr_h = nc.dram_tensor("L0_ffn_w_freq_real", [D, D], F32, kind="ExternalInput").ap()
    wfi_h = nc.dram_tensor("L0_ffn_w_freq_imag", [D, D], F32, kind="ExternalInput").ap()
    brows_h = nc.dram_tensor("brows", [1, 3 * D], F32, kind="ExternalInput").ap()
    c16_h = nc.dram_tensor("C16", [D, D], F16, kind="ExternalInput").ap()
    s16_h = nc.dram_tensor("S16", [D, D], F16, kind="ExternalInput").ap()
    id16_h = nc.dram_tensor("ident16", [128, 128], F16, kind="ExternalInput").ap()
    ones16_h = nc.dram_tensor("ones16", [1, 512], F16, kind="ExternalInput").ap()
    crole_h = nc.dram_tensor("Crole", [D, 512], F16, kind="ExternalInput").ap()
    srole_h = nc.dram_tensor("Srole", [D, 512], F16, kind="ExternalInput").ap()
    midx_h = nc.dram_tensor("midx", [128, 4], I32, kind="ExternalInput").ap()
    hidx_h = nc.dram_tensor("hidx", [128, 1], I32, kind="ExternalInput").ap()
    fidx_h = nc.dram_tensor("fidx", [2, 2], I32, kind="ExternalInput").ap()
    nonce_h = nc.dram_tensor("nonce", [2, 1], I32, kind="ExternalInput").ap()
    rmask_h = nc.dram_tensor("rmask", [128, 1], F32, kind="ExternalInput").ap()
    out_h = nc.dram_tensor("out", [TOK, D], F32, kind="ExternalOutput").ap()
    if dbg:
        dbg_pre = nc.dram_tensor("dbg_pre", [128, 4096], F16,
                                 kind="ExternalOutput").ap()
        dbg_xnt = nc.dram_tensor("dbg_xnt", [128, 4096], F16,
                                 kind="ExternalOutput").ap()
        dbg_trb = nc.dram_tensor("dbg_trb", [128, 1024], F16,
                                 kind="ExternalOutput").ap()
        dbg_os = nc.dram_tensor("dbg_os", [128, 1024], F32,
                                 kind="ExternalOutput").ap()

    sh_m = nc.dram_tensor("sh_m", [D, D], F16, addr_space="Shared").ap()
    sh_h = nc.dram_tensor("sh_h", [256, 8], F32, addr_space="Shared").ap()
    sh_flag = nc.dram_tensor("sh_flag", [16, 1], I32, addr_space="Shared").ap()

    A = {}

    def sb(name, shape, dtype):
        A[name] = nc.alloc_sbuf_tensor(name, shape, dtype).ap()
        return A[name]

    sb("sb_ident16", [128, 128], F16)
    sb("brows16", [1, 3 * D], F16)
    sb("sb_ones16", [1, 512], F16)
    sb("ones_i32", [1, 8], I32)
    sb("flag_rd", [1, 8], I32)
    sb("wz16", [128, KC * D], F16)
    sb("wh16", [128, KC * D], F16)
    ar = nc.alloc_sbuf_tensor("arena", [128, 4 * 8192], F16).ap()
    WFR16 = ar[:, 0:8192]
    WFI16 = ar[:, 8192:16384]
    C16 = ar[:, 0:8192]
    S16 = ar[:, 8192:16384]
    CROLE = ar[:, 16384:20480]
    SROLE = ar[:, 20480:24576]
    M16 = ar[:, 16384:24576]
    PT2 = ar[:, 24576:28672]
    T2T2 = ar[:, 28672:32768]
    G16 = ar[:, 24576:32768]
    sb("wstage", [128, D], F32)
    sb("xpool", [128, NXS * D], F32)
    sb("xasm", [128, 2 * D], F32)
    sb("xn", [128, 2 * D], F16)
    sb("xnT", [128, 2 * 4096], F16)
    sb("pre", [128, 2 * NF * 512], F16)
    sb("sumc", [128, NT], F32)
    sb("ssqc", [128, NT], F32)
    sb("meanc", [128, NT], F32)
    sb("varc", [128, NT], F32)
    sb("stdc", [128, NT], F32)
    sb("rstdc", [128, NT], F32)
    sb("zb", [128, 512], F32)
    sb("ab", [128, 2 * 512], F32)
    sb("bb", [128, 2 * 512], F32)
    sb("hb", [128, 2 * 512], F32)
    sb("t0b", [128, 512], F16)
    sb("zeros", [128, 512], F16)
    sb("hcarry", [128, 8], F32)
    sb("sb_midx", [128, 4], I32)
    sb("sb_hidx", [128, 1], I32)
    sb("sb_fidx", [2, 2], I32)
    sb("sb_nonce", [2, 1], I32)
    sb("sb_rmask", [128, 1], F32)
    sb("flag_rd2", [1, 16], I32)
    sb("gcarry", [128, 8], F32)
    sb("hin", [128, 8], F32)
    sb("tiny", [128, 8], F32)
    sb("polljunk", [128, 128], F32)
    sb("outst", [128, 2 * D], F32)
    sb("mtmp", [128, D], F16)
    sb("hpark", [128, 4096], F16)

    JUNK = A["outst"][:, 0:D]

    PS = {}

    def ps(name, shape, dtype):
        PS[name] = nc.alloc_psum_tensor(name, shape, dtype).ap()
        return PS[name]

    ps("uB0", [128, 512], F32)
    ps("uB1", [128, 512], F32)
    ps("hB0", [128, 512], F32)
    ps("hB1", [128, 512], F32)
    ps("yB0", [128, 512], F32)
    ps("yB1", [128, 512], F32)
    ps("trA", [128, 1024], F16)
    ps("trB", [128, 1024], F16)

    POLLSEM = [None]
    sched = Sched()
    E = sched.emit

    def xslot(i):
        return A["xpool"][:, (i % NXS) * D:((i % NXS) + 1) * D]

    def xbuf(i):
        return buf("xpool", (i % NXS) * D, ((i % NXS) + 1) * D)

    def xaslot(i):
        return A["xasm"][:, (i % 2) * D:((i % 2) + 1) * D]

    def xabuf(i):
        return buf("xasm", (i % 2) * D, ((i % 2) + 1) * D)

    PTv = PT2
    T2v = T2T2

    def ld(dst, src, wb, tag, eng="sync"):
        E(eng, lambda e, s_, d=dst, s=src: e.dma_start(out=d, in_=s),
          writes=(wb,), tag=tag, dma=True)

    ld(A["sb_ident16"], id16_h, buf("sb_ident16"), "ld_id")
    ld(A["sb_ones16"], ones16_h, buf("sb_ones16"), "ld_on")

    E("gpsimd", lambda e, s_: e.memset(A["zeros"], 0.0), writes=(buf("zeros"),), tag="mz")
    E("gpsimd", lambda e, s_: e.memset(A["hcarry"], 0.0), writes=(buf("hcarry"),), tag="mh")
    E("gpsimd", lambda e, s_: e.memset(A["gcarry"], 1.0), writes=(buf("gcarry"),), tag="mg")
    E("gpsimd", lambda e, s_: e.memset(A["ones_i32"], 1), writes=(buf("ones_i32"),), tag="mo")
    for bi, (lo, hi) in enumerate(((0, 1024), (1024, 2048), (2048, 3072))):
        E("scalar", lambda e, s_, lo=lo, hi=hi: e.dma_start(
            out=A["wstage"][0:1, 0:hi - lo], in_=brows_h[0:1, lo:hi]),
          writes=(buf("wstage", 0, hi - lo),), tag=f"ld_br{bi}", dma=True)
        E("vector", lambda e, s_, lo=lo, hi=hi: e.tensor_copy(
            A["brows16"][0:1, lo:hi], A["wstage"][0:1, 0:hi - lo]),
          reads=(buf("wstage", 0, hi - lo),),
          writes=(buf("brows16", lo, hi),), tag=f"cbr{bi}")

    def cast_weight(h_ap, dst16, region, base, tag, q="sync"):
        sl = 0 if q == "sync" else 1
        stg = A["wstage"][:, sl * 512:(sl + 1) * 512]
        stb = buf("wstage", sl * 512, (sl + 1) * 512)
        for k in range(KC):
            for hh in range(2):
                E(q, lambda e, s_, k=k, hh=hh, h=h_ap, stg=stg: e.dma_start(
                    out=stg, in_=h[128 * k:128 * (k + 1),
                                   512 * hh:512 * (hh + 1)]),
                  writes=(stb,), tag=f"{tag}l{k}_{hh}", dma=True)
                E("vector", lambda e, s_, k=k, hh=hh, d=dst16, stg=stg:
                  e.tensor_copy(
                    d[:, k * D + 512 * hh:k * D + 512 * (hh + 1)], stg),
                  reads=(stb,),
                  writes=(buf(region, base + k * D + 512 * hh,
                              base + k * D + 512 * (hh + 1)),),
                  tag=f"{tag}c{k}_{hh}")

    def emit_weight_loads_a():
        cast_weight(wz_h, A["wz16"], "wz16", 0, "wz", q="gpsimd")
        cast_weight(wh_h, A["wh16"], "wh16", 0, "wh", q="sync")

    def emit_weight_loads_b():
        cast_weight(wfi_h, WFI16, "arena", 8192, "wi", q="gpsimd")
        cast_weight(wfr_h, WFR16, "arena", 0, "wr", q="sync")
        for k in range(KC):
            ld(CROLE[:, k * 512:(k + 1) * 512],
               crole_h[128 * k:128 * (k + 1), :],
               buf("arena", 16384 + k * 512, 16384 + (k + 1) * 512),
               f"lCr{k}", eng="scalar")
            ld(SROLE[:, k * 512:(k + 1) * 512],
               srole_h[128 * k:128 * (k + 1), :],
               buf("arena", 20480 + k * 512, 20480 + (k + 1) * 512),
               f"lSr{k}", eng="gpsimd")
    for dst, src, wb, tg in ((A["sb_midx"], midx_h, buf("sb_midx"), "lmi"),
                             (A["sb_hidx"], hidx_h, buf("sb_hidx"), "lhi"),
                             (A["sb_fidx"], fidx_h, buf("sb_fidx"), "lfi"),
                             (A["sb_nonce"], nonce_h, buf("sb_nonce"), "lno"),
                             (A["sb_rmask"], rmask_h, buf("sb_rmask"), "lrm")):
        ld(dst, src, wb, tg)

    def emit_mbuild():
        slot0 = (buf("arena", 0, 8192),)
        slot1 = (buf("arena", 8192, 16384),)
        slot2 = (buf("arena", 16384, 24576),)
        T1STG = A["outst"][:, D:D + 512]

        for oc in range(KC):
            pt_bank = PS["yB0"] if oc % 2 == 0 else PS["yB1"]
            t1_bank = PS["uB0"] if oc % 2 == 0 else PS["uB1"]
            t2_bank = PS["hB0"] if oc % 2 == 0 else PS["hB1"]
            pt_bb = buf("yB0" if oc % 2 == 0 else "yB1")
            t1_bb = buf("uB0" if oc % 2 == 0 else "uB1")
            t2_bb = buf("hB0" if oc % 2 == 0 else "hB1")

            def s1(e, s_, oc=oc, pt=pt_bank, t1=t1_bank, t2=t2_bank):
                for k in range(KC):
                    e.matmul(pt, WFR16[:, k * D + 128 * oc:k * D + 128 * (oc + 1)],
                             CROLE[:, k * 512:(k + 1) * 512],
                             start=(k == 0), stop=False)
                for k in range(KC):
                    e.matmul(pt, WFI16[:, k * D + 128 * oc:k * D + 128 * (oc + 1)],
                             SROLE[:, k * 512:(k + 1) * 512],
                             start=False, stop=(k == KC - 1))
                for k in range(KC):
                    e.matmul(t1, WFI16[:, k * D + 128 * oc:k * D + 128 * (oc + 1)],
                             CROLE[:, k * 512:(k + 1) * 512],
                             start=(k == 0), stop=(k == KC - 1))
                last = None
                for k in range(KC):
                    last = e.matmul(
                        t2, WFR16[:, k * D + 128 * oc:k * D + 128 * (oc + 1)],
                        SROLE[:, k * 512:(k + 1) * 512],
                        start=(k == 0), stop=(k == KC - 1))
                return last
            E("tensor", s1, reads=slot0 + slot1 + slot2,
              writes=(pt_bb, t1_bb, t2_bb), tag=f"ms1_{oc}")
            E("scalar", lambda e, s_, oc=oc, pt=pt_bank: e.activation(
                PTv[:, oc * 512:(oc + 1) * 512], pt, AF.Identity),
              reads=(pt_bb,),
              writes=(buf("arena", 24576 + oc * 512, 24576 + (oc + 1) * 512),),
              tag=f"ePT{oc}")
            E("vector", lambda e, s_, t1=t1_bank: e.tensor_copy(T1STG, t1),
              reads=(t1_bb,), writes=(buf("outst", D, D + 512),),
              tag=f"eT1{oc}")
            E("vector", lambda e, s_, oc=oc, t2=t2_bank: e.tensor_tensor(
                T2v[:, oc * 512:(oc + 1) * 512], t2, T1STG, ALU.subtract),
              reads=(t2_bb, buf("outst", D, D + 512)),
              writes=(buf("arena", 28672 + oc * 512, 28672 + (oc + 1) * 512),),
              tag=f"eQn{oc}")

        for k in range(KC):
            ld(C16[:, k * D:(k + 1) * D], c16_h[128 * k:128 * (k + 1), :],
               buf("arena", k * D, (k + 1) * D), f"lC{k}", eng="scalar")
            ld(S16[:, k * D:(k + 1) * D], s16_h[128 * k:128 * (k + 1), :],
               buf("arena", 8192 + k * D, 8192 + (k + 1) * D), f"lS{k}",
               eng="gpsimd")

        bankmap = {(0, 0): "uB0", (0, 1): "uB1", (1, 0): "hB0", (1, 1): "hB1",
                   (2, 0): "yB0", (2, 1): "yB1", (3, 0): "uB0", (3, 1): "uB1"}
        for wave, rcs in ((0, (0, 1)), (1, (2, 3))):
            for oc in range(KC):
                def s2r(e, s_, rcs=rcs, oc=oc):
                    last = None
                    for rc in rcs:
                        for half in range(2):
                            bank = PS[bankmap[(rc, half)]]
                            hs = 512 * half
                            e.matmul(
                                bank,
                                PTv[:, oc * 512 + 128 * rc:oc * 512 + 128 * (rc + 1)],
                                C16[:, oc * D + hs:oc * D + hs + 512],
                                start=(oc == 0), stop=False)
                            last = e.matmul(
                                bank,
                                T2v[:, oc * 512 + 128 * rc:oc * 512 + 128 * (rc + 1)],
                                S16[:, oc * D + hs:oc * D + hs + 512],
                                start=False, stop=(oc == KC - 1))
                    return last
                E("tensor", s2r,
                  reads=(buf("arena", oc * D, (oc + 1) * D),
                         buf("arena", 8192 + oc * D, 8192 + (oc + 1) * D),
                         buf("arena", 24576 + oc * 512, 24576 + (oc + 1) * 512),
                         buf("arena", 28672 + oc * 512, 28672 + (oc + 1) * 512)),
                  writes=tuple(buf(bankmap[(rc, h)]) for rc in rcs
                               for h in range(2)),
                  tag=f"ms2w{wave}_{oc}")
            for rc in rcs:
                for half in range(2):
                    bnm = bankmap[(rc, half)]
                    E("scalar", lambda e, s_, half=half, bnm=bnm: e.activation(
                        A["mtmp"][:, half * 512:(half + 1) * 512], PS[bnm],
                        AF.Identity, scale=1.0 / D),
                      reads=(buf(bnm),),
                      writes=(buf("mtmp", half * 512, (half + 1) * 512),),
                      tag=f"eM{rc}_{half}")
                E("gpsimd", lambda e, s_, rc=rc: e.indirect_dma_start(
                    sh_m, IndirectOffsetOnAxis(A["sb_midx"][:, rc:rc + 1], 0),
                    A["mtmp"], None),
                  reads=(buf("mtmp", 0, D), buf("sb_midx", rc, rc + 1)),
                  writes=(buf("shm", 128 * rc, 128 * (rc + 1)),), tag=f"wM{rc}",
                  dma=True)

        def m_flag_poll(e, s_):
            e.indirect_dma_start(
                sh_flag, IndirectOffsetOnAxis(A["sb_fidx"][:, 0:1], 0),
                A["sb_nonce"], None).then_inc(POLLSEM[0], 16)
            with e.register("m_rgo") as rgo, e.register("m_rn") as rn, \
                 e.register("m_r0") as r0, e.register("m_r1") as r1:
                e.reg_load(rn, A["sb_nonce"][0:1, 0:1])
                e.reg_mov(rgo, 1)
                with e.While(rgo):
                    if audit:
                        for _d in range(6):
                            e.nop()
                    else:
                        for _d in range(24):
                            e.tensor_copy(A["polljunk"], A["zeros"][:, 0:128])
                    e.dma_start(out=A["flag_rd2"],
                                in_=sh_flag).then_inc(POLLSEM[0], 16)
                    e.nop()
                    e.reg_load(r0, A["flag_rd2"][0:1, 0:1])
                    e.reg_load(r1, A["flag_rd2"][0:1, 1:2])
                    e.reg_alu(r0, r0, rn, ALU.is_equal)
                    e.reg_alu(r1, r1, rn, ALU.is_equal)
                    e.reg_alu(r0, r0, r1, ALU.add)
                    e.reg_alu(rgo, r0, 2, ALU.is_lt)
            return e.nop()
        E("gpsimd", m_flag_poll,
          reads=(buf("shm", 0, 512), buf("sb_nonce"), buf("sb_fidx", 0, 1)),
          writes=(buf("flag_rd2"),), tag="mpoll")
        for k in range(KC):
            E("gpsimd" if k % 2 == 0 else "scalar",
              lambda e, s_, k=k: e.dma_start(
                out=M16[:, k * D:(k + 1) * D],
                in_=sh_m[128 * k:128 * (k + 1), :]),
              reads=(buf("flag_rd2"),),
              writes=(buf("arena", 16384 + k * D, 16384 + (k + 1) * D),),
              tag=f"ldM{k}", dma=True)
        for f in range(NF):
            E("gpsimd", lambda e, s_, f=f: e.tensor_copy(
                G16[:, f * 512:(f + 1) * 512],
                A["pre"][:, 4096 + f * 512:4096 + (f + 1) * 512]),
              reads=(buf("pre", 4096 + f * 512, 4096 + (f + 1) * 512),),
              writes=(buf("arena", 24576 + f * 512, 24576 + (f + 1) * 512),),
              tag=f"gcp{f}")

    def emit_tile_front(i):
        g, t = i // 4, i % 4
        gs = (g % 2) * 4096
        ld(xslot(i), x_h[128 * i:128 * (i + 1), :], xbuf(i), f"mx{i}")
        E("scalar", lambda e, s_, i=i: e.activation(
            JUNK, xslot(i), AF.Identity, bias=0.0, scale=1.0,
            accum_out=A["sumc"][:, i:i + 1]),
          reads=(xbuf(i),), writes=(buf("sumc", i, i + 1), buf("outst", 0, D)),
          tag=f"sum{i}")
        E("scalar", lambda e, s_, i=i: e.activation(
            JUNK, xslot(i), AF.Square, bias=0.0, scale=1.0,
            accum_out=A["ssqc"][:, i:i + 1]),
          reads=(xbuf(i),), writes=(buf("ssqc", i, i + 1), buf("outst", 0, D)),
          tag=f"ssq{i}")

        E("vector", lambda e, s_, i=i: e.tensor_scalar(
            A["meanc"][:, i:i + 1], A["sumc"][:, i:i + 1], 1.0 / D, None,
            ALU.mult),
          reads=(buf("sumc", i, i + 1),), writes=(buf("meanc", i, i + 1),),
          tag=f"vm{i}")
        E("vector", lambda e, s_, i=i: e.tensor_tensor(
            A["stdc"][:, i:i + 1], A["meanc"][:, i:i + 1],
            A["meanc"][:, i:i + 1], ALU.mult),
          reads=(buf("meanc", i, i + 1),), writes=(buf("stdc", i, i + 1),),
          tag=f"vq{i}")
        E("vector", lambda e, s_, i=i: e.scalar_tensor_tensor(
            A["varc"][:, i:i + 1], A["ssqc"][:, i:i + 1], 1.0 / D,
            A["stdc"][:, i:i + 1], ALU.mult, ALU.subtract),
          reads=(buf("ssqc", i, i + 1), buf("stdc", i, i + 1)),
          writes=(buf("varc", i, i + 1),), tag=f"vv{i}")
        E("vector", lambda e, s_, i=i: e.tensor_scalar(
            A["varc"][:, i:i + 1], A["varc"][:, i:i + 1], EPS, None, ALU.add),
          reads=(buf("varc", i, i + 1),), writes=(buf("varc", i, i + 1),),
          tag=f"ve{i}")
        E("scalar", lambda e, s_, i=i: e.activation(
            A["stdc"][:, i:i + 1], A["varc"][:, i:i + 1], AF.Sqrt),
          reads=(buf("varc", i, i + 1),), writes=(buf("stdc", i, i + 1),),
          tag=f"sq{i}")
        E("vector", lambda e, s_, i=i: e.reciprocal(
            A["rstdc"][:, i:i + 1], A["stdc"][:, i:i + 1]),
          reads=(buf("stdc", i, i + 1),), writes=(buf("rstdc", i, i + 1),),
          tag=f"rs{i}")
        xnv = A["xn"][:, (i % 2) * D:((i % 2) + 1) * D]
        xnb = buf("xn", (i % 2) * D, ((i % 2) + 1) * D)
        E("vector", lambda e, s_, i=i, xnv=xnv: e.tensor_scalar(
            xnv, xslot(i), A["meanc"][:, i:i + 1], A["rstdc"][:, i:i + 1],
            ALU.subtract, ALU.mult),
          reads=(xbuf(i), buf("meanc", i, i + 1), buf("rstdc", i, i + 1)),
          writes=(xnb,), tag=f"xn{i}")

        def trx(e, s_, xnv=xnv):
            last = None
            for k in range(KC):
                last = e.transpose(PS["trA"][:, 128 * k:128 * (k + 1)],
                                   xnv[:, 128 * k:128 * (k + 1)], A["sb_ident16"])
            return last
        E("tensor", trx, reads=(xnb, buf("sb_ident16")),
          writes=(buf("trA"),), tag=f"trx{i}")

        def exn(e, s_, gs=gs, t=t):
            out3 = A["xnT"][:, gs:gs + 4096].rearrange(
                "p (k q) -> p k q", k=KC)[:, :, t * 128:(t + 1) * 128]
            in3 = PS["trA"].rearrange("p (k q) -> p k q", k=KC)
            return e.copy(out3, in3)
        E("scalar", exn, reads=(buf("trA"),),
          writes=tuple(buf("xnT", gs + k * 512 + t * 128,
                           gs + k * 512 + (t + 1) * 128) for k in range(KC)),
          tag=f"exn{i}")

    def emit_group_f(g, f, phase="all"):
        gs = (g % 2) * 4096
        xnT_read = buf("xnT", gs, gs + 4096)
        ub = PS["uB0"] if f % 2 == 0 else PS["uB1"]
        hbk = PS["hB0"] if f % 2 == 0 else PS["hB1"]
        yb = PS["yB0"] if f % 2 == 0 else PS["yB1"]
        ubb = buf("uB0" if f % 2 == 0 else "uB1")
        hbb = buf("hB0" if f % 2 == 0 else "hB1")
        ybb = buf("yB0" if f % 2 == 0 else "yB1")
        fs = (f % 2) * 512

        def mmproj(e, w16, bias_off, bank, gs=gs, f=f):
            e.matmul(bank,
                     A["brows16"][:, bias_off + f * 128:bias_off + (f + 1) * 128],
                     A["sb_ones16"], start=True, stop=False)
            last = None
            for k in range(KC):
                last = e.matmul(
                    bank, w16[:, k * D + 128 * f:k * D + 128 * (f + 1)],
                    A["xnT"][:, gs + k * 512:gs + (k + 1) * 512],
                    start=False, stop=(k == KC - 1))
            return last

        def emit_y():
            E("tensor", lambda e, s_: mmproj(e, M16, 2 * D, yb),
              reads=(xnT_read, buf("arena", 16384, 24576), buf("brows16"),
                     buf("sb_ones16")),
              writes=(ybb,), tag=f"y{g}_{f}")
            E("scalar", lambda e, s_: e.activation(
                A["t0b"], yb, AF.Identity),
              reads=(ybb,), writes=(buf("t0b"),), tag=f"t0{g}_{f}")
            poff = (g % 2) * 4096 + f * 512
            if phase == "y" and g == 0:
                E("gpsimd", lambda e, s_, poff=poff: e.tensor_tensor(
                    A["pre"][:, poff:poff + 512],
                    A["pre"][:, poff:poff + 512], A["t0b"], ALU.add),
                  reads=(buf("t0b"), buf("pre", poff, poff + 512)),
                  writes=(buf("pre", poff, poff + 512),), tag=f"pre{g}_{f}")
            elif phase == "y":
                hoff = f * 512
                E("gpsimd", lambda e, s_, poff=poff, hoff=hoff: e.tensor_tensor(
                    A["pre"][:, poff:poff + 512], A["t0b"],
                    A["hpark"][:, hoff:hoff + 512], ALU.add),
                  reads=(buf("t0b"), buf("hpark", hoff, hoff + 512)),
                  writes=(buf("pre", poff, poff + 512),), tag=f"pre{g}_{f}")
            else:
                E("gpsimd", lambda e, s_, poff=poff, fs=fs: e.tensor_tensor(
                    A["pre"][:, poff:poff + 512], A["t0b"],
                    A["hb"][:, fs:fs + 512], ALU.add),
                  reads=(buf("t0b"), buf("hb", fs, fs + 512)),
                  writes=(buf("pre", poff, poff + 512),), tag=f"pre{g}_{f}")

        if phase == "y":
            emit_y()
            return

        E("tensor", lambda e, s_: mmproj(e, A["wz16"], 0, ub),
          reads=(xnT_read, buf("wz16"), buf("brows16"), buf("sb_ones16")),
          writes=(ubb,), tag=f"u{g}_{f}")
        E("scalar", lambda e, s_: e.activation(A["zb"], ub, AF.Sigmoid),
          reads=(ubb,), writes=(buf("zb"),), tag=f"z{g}_{f}")
        E("scalar", lambda e, s_: e.activation(
            A["ab"][:, fs:fs + 512], ub, AF.Sigmoid, scale=-1.0),
          reads=(ubb,), writes=(buf("ab", fs, fs + 512),), tag=f"a{g}_{f}")
        E("tensor", lambda e, s_: mmproj(e, A["wh16"], D, hbk),
          reads=(xnT_read, buf("wh16"), buf("brows16"), buf("sb_ones16")),
          writes=(hbb,), tag=f"h{g}_{f}")
        E("vector", lambda e, s_: e.tensor_tensor(
            A["bb"][:, fs:fs + 512], A["zb"], hbk, ALU.mult),
          reads=(buf("zb"), hbb), writes=(buf("bb", fs, fs + 512),),
          tag=f"b{g}_{f}")

        E("vector", lambda e, s_, f=f, fs=fs: e.tensor_tensor_scan(
            A["hb"][:, fs:fs + 512], A["ab"][:, fs:fs + 512],
            A["bb"][:, fs:fs + 512], A["hcarry"][:, f:f + 1],
            ALU.mult, ALU.add),
          reads=(buf("ab", fs, fs + 512), buf("bb", fs, fs + 512),
                 buf("hcarry", f, f + 1)),
          writes=(buf("hb", fs, fs + 512),), tag=f"sh{g}_{f}")
        E("vector", lambda e, s_, f=f, fs=fs: e.tensor_copy(
            A["hcarry"][:, f:f + 1], A["hb"][:, fs + 511:fs + 512]),
          reads=(buf("hb", fs, fs + 512),),
          writes=(buf("hcarry", f, f + 1),), tag=f"shc{g}_{f}")

        if g < 1:
            goff = f * 512
            gv = A["pre"][:, 4096 + goff:4096 + goff + 512]
            E("vector", lambda e, s_, f=f, fs=fs, gv=gv: e.tensor_tensor_scan(
                gv, A["ab"][:, fs:fs + 512],
                A["zeros"], A["gcarry"][:, f:f + 1], ALU.mult, ALU.add),
              reads=(buf("ab", fs, fs + 512), buf("zeros"),
                     buf("gcarry", f, f + 1)),
              writes=(buf("pre", 4096 + goff, 4096 + goff + 512),),
              tag=f"sg{g}_{f}")

        if phase == "uh":
            if g == 0:
                poff0 = f * 512
                E("gpsimd", lambda e, s_, poff0=poff0, fs=fs: e.tensor_copy(
                    A["pre"][:, poff0:poff0 + 512], A["hb"][:, fs:fs + 512]),
                  reads=(buf("hb", fs, fs + 512),),
                  writes=(buf("pre", poff0, poff0 + 512),), tag=f"preh{g}_{f}")
            else:
                hoff = f * 512
                E("gpsimd", lambda e, s_, hoff=hoff, fs=fs: e.tensor_copy(
                    A["hpark"][:, hoff:hoff + 512], A["hb"][:, fs:fs + 512]),
                  reads=(buf("hb", fs, fs + 512),),
                  writes=(buf("hpark", hoff, hoff + 512),), tag=f"preh{g}_{f}")
            return
        emit_y()

    def emit_tile_back(i):
        g, t = i // 4, i % 4
        poffg = (g % 2) * 4096
        ld(xaslot(i), x_h[128 * i:128 * (i + 1), :], xabuf(i), f"ax{i}",
           eng="gpsimd" if i % 2 == 0 else "scalar")

        def trp(e, s_, t=t, poffg=poffg):
            last = None
            for f in range(NF):
                last = e.transpose(
                    PS["trB"][:, 128 * f:128 * (f + 1)],
                    A["pre"][:, poffg + f * 512 + t * 128:
                             poffg + f * 512 + (t + 1) * 128],
                    A["sb_ident16"])
            return last
        E("tensor", trp,
          reads=(buf("pre", poffg, poffg + 4096), buf("sb_ident16")),
          writes=(buf("trB"),), tag=f"trp{i}")
        osl = (i % 2) * D
        if dbg and i == 4:
            E("scalar", lambda e, s_: e.activation(
                A["mtmp"], PS["trB"], AF.Identity),
              reads=(buf("trB"),), writes=(buf("mtmp", 0, D),), tag="dbgtc")
            E("scalar", lambda e, s_: e.dma_start(
                out=dbg_trb, in_=A["mtmp"]),
              reads=(buf("mtmp", 0, D),), writes=(buf("dbgt"),), tag="dbgt",
              dma=True)
        E("vector", lambda e, s_, i=i, osl=osl: e.tensor_tensor(
            A["outst"][:, osl:osl + D], xaslot(i), PS["trB"], ALU.add),
          reads=(xabuf(i), buf("trB")),
          writes=(buf("outst", osl, osl + D),), tag=f"oadd{i}")
        if dbg and i == 4:
            E("scalar", lambda e, s_, osl=osl: e.dma_start(
                out=dbg_os, in_=A["outst"][:, osl:osl + D]),
              reads=(buf("outst", osl, osl + D),), writes=(buf("dbgo"),),
              tag="dbgo", dma=True)
        E("scalar", lambda e, s_, i=i, osl=osl: e.dma_start(
            out=out_h[128 * i:128 * (i + 1), :],
            in_=A["outst"][:, osl:osl + D]),
          reads=(buf("outst", osl, osl + D),),
          writes=(buf("outhbm", 128 * i, 128 * (i + 1)),), tag=f"st{i}",
          dma=True)

    for t in range(4):
        emit_tile_front(t)
    emit_weight_loads_a()
    for f in range(NF):
        emit_group_f(0, f, "uh")
    for t in range(4):
        emit_tile_front(4 + t)
    emit_weight_loads_b()
    for f in range(NF):
        emit_group_f(1, f, "uh")
    emit_mbuild()
    for f in range(NF):
        emit_group_f(0, f, "y")
    for t in range(4):
        emit_tile_front(8 + t)
    for f in range(NF):
        emit_group_f(1, f, "y")
    for t in range(4):
        emit_tile_back(t)
    for g in range(2, NG):
        if g < NG - 1:
            for t in range(4):
                emit_tile_front(4 * (g + 1) + t)
        for f in range(NF):
            emit_group_f(g, f)
        if dbg and g == 1:
            E("scalar", lambda e, s_: e.dma_start(
                out=dbg_pre, in_=A["pre"][:, 4096:8192]),
              reads=(buf("pre", 4096, 8192),), writes=(buf("dbgp"),),
              tag="dbgp", dma=True)
            E("scalar", lambda e, s_: e.dma_start(
                out=dbg_xnt, in_=A["xnT"][:, 4096:8192]),
              reads=(buf("xnT", 4096, 8192),), writes=(buf("dbgx"),),
              tag="dbgx", dma=True)
        for t in range(4):
            emit_tile_back(4 * (g - 1) + t)
    for t in range(4):
        emit_tile_back(4 * 7 + t)

    E("gpsimd", lambda e, s_: e.indirect_dma_start(
        sh_h, IndirectOffsetOnAxis(A["sb_hidx"], 0), A["hcarry"], None),
      reads=(buf("hcarry", 0, 8), buf("sb_hidx")),
      writes=(buf("shh"),), tag="hxw", dma=True)

    def h_flag_poll(e, s_):
        e.indirect_dma_start(
            sh_flag, IndirectOffsetOnAxis(A["sb_fidx"][:, 1:2], 0),
            A["sb_nonce"], None).then_inc(POLLSEM[0], 16)
        with e.register("h_rgo") as rgo, e.register("h_rn") as rn, \
             e.register("h_r0") as r0, e.register("h_r1") as r1:
            e.reg_load(rn, A["sb_nonce"][0:1, 0:1])
            e.reg_mov(rgo, 1)
            with e.While(rgo):
                if audit:
                    for _d in range(6):
                        e.nop()
                else:
                    for _d in range(24):
                        e.tensor_copy(A["polljunk"], A["zeros"][:, 0:128])
                e.dma_start(out=A["flag_rd2"],
                            in_=sh_flag).then_inc(POLLSEM[0], 16)
                e.nop()
                e.reg_load(r0, A["flag_rd2"][0:1, 2:3])
                e.reg_load(r1, A["flag_rd2"][0:1, 3:4])
                e.reg_alu(r0, r0, rn, ALU.is_equal)
                e.reg_alu(r1, r1, rn, ALU.is_equal)
                e.reg_alu(r0, r0, r1, ALU.add)
                e.reg_alu(rgo, r0, 2, ALU.is_lt)
        return e.nop()
    E("gpsimd", h_flag_poll,
      reads=(buf("shh"), buf("sb_nonce"), buf("sb_fidx", 1, 2)),
      writes=(buf("flag_rd2"),), tag="hpoll")
    E("gpsimd", lambda e, s_: e.dma_start(out=A["tiny"], in_=sh_h[0:128, :]),
      reads=(buf("flag_rd2"),), writes=(buf("tiny"),), tag="hrd", dma=True)
    E("vector", lambda e, s_: e.tensor_scalar_mul(
        A["hin"], A["tiny"], A["sb_rmask"]),
      reads=(buf("tiny"), buf("sb_rmask")), writes=(buf("hin"),), tag="hmask")

    CORR = A["mtmp"]
    for i in range(4):
        g, t = i // 4, i % 4
        E("gpsimd", lambda e, s_, i=i: e.dma_start(
            out=xaslot(i), in_=out_h[128 * i:128 * (i + 1), :]),
          reads=(buf("outhbm", 128 * i, 128 * (i + 1)),),
          writes=(xabuf(i),), tag=f"rb{i}", dma=True)

        def mkcorr(e, s_, g=g, t=t):
            last = None
            for f in range(NF):
                goff = (g * 8 + f) * 512 + t * 128
                last = e.tensor_scalar_mul(
                    CORR[:, f * 128:(f + 1) * 128],
                    G16[:, goff:goff + 128], A["hin"][:, f:f + 1])
            return last
        E("vector", mkcorr, reads=(buf("arena", 24576, 28672), buf("hin")),
          writes=(buf("mtmp", 0, D),), tag=f"corr{i}")

        def trc(e, s_):
            last = None
            for f in range(NF):
                last = e.transpose(PS["trB"][:, 128 * f:128 * (f + 1)],
                                   CORR[:, f * 128:(f + 1) * 128],
                                   A["sb_ident16"])
            return last
        E("tensor", trc, reads=(buf("mtmp", 0, D), buf("sb_ident16")),
          writes=(buf("trB"),), tag=f"trc{i}")
        osl = (i % 2) * D
        E("vector", lambda e, s_, i=i, osl=osl: e.tensor_tensor(
            A["outst"][:, osl:osl + D], xaslot(i), PS["trB"], ALU.add),
          reads=(xabuf(i), buf("trB")),
          writes=(buf("outst", osl, osl + D),), tag=f"o3_{i}")
        E("scalar", lambda e, s_, i=i, osl=osl: e.dma_start(
            out=out_h[128 * i:128 * (i + 1), :],
            in_=A["outst"][:, osl:osl + D]),
          reads=(buf("outst", osl, osl + D),),
          writes=(buf("outhbm", 128 * i, 128 * (i + 1)),), tag=f"s3_{i}",
          dma=True)

    waits, drains = resolve(sched)
    per_engine = {e: [] for e in ENGS}
    per_engine_idx = {e: [] for e in ENGS}
    for gi, (item, w) in enumerate(zip(sched.items, waits)):
        per_engine[item[0]].append((item, w))
        per_engine_idx[item[0]].append(gi)

    from contextlib import ExitStack
    with ExitStack() as stack:
        sems = {k: stack.enter_context(nc.semaphore(f"sem_{k}")) for k in SEMKEYS}
        sems["poll"] = stack.enter_context(nc.semaphore("sem_poll"))
        POLLSEM[0] = sems["poll"]
        block = stack.enter_context(nc.Block())

        def emit_stream(eng_name, eng):
            hw = {k: 0 for k in SEMKEYS}
            for gi, ((engine, fn, reads, writes, tag, ev), w) in zip(
                    per_engine_idx[eng_name], per_engine[eng_name]):
                dr = drains.get(gi, {})
                for dkey2, dval in sorted(dr.items()):
                    if dval > hw[dkey2]:
                        eng.wait_ge(sems[dkey2], dval)
                        hw[dkey2] = dval
                for wkey, wval in sorted(w.items()):
                    if wval > hw[wkey]:
                        eng.wait_ge(sems[wkey], wval)
                        hw[wkey] = wval
                last = fn(eng, sems[ev[0]])
                if not ev[3]:
                    assert last is not None, f"item {tag} returned None"
                    last.then_inc(sems[ev[0]], ev[2])
            if eng_name == "gpsimd":
                eng.wait_ge(sems["scalar_c"], sched.counts["scalar_c"])
                eng.wait_ge(sems["scalar_d"], sched.counts["scalar_d"])
                eng.wait_ge(sems["sync_d"], sched.counts["sync_d"])
                eng.wait_ge(sems["vector_d"], sched.counts["vector_d"])

        @block.sync
        def _(sync):
            emit_stream("sync", sync)

        @block.tensor
        def _(tensor):
            emit_stream("tensor", tensor)

        @block.vector
        def _(vector):
            emit_stream("vector", vector)

        @block.scalar
        def _(scalar):
            emit_stream("scalar", scalar)

        @block.gpsimd
        def _(gpsimd):
            emit_stream("gpsimd", gpsimd)

    return nc


_NC_CACHE = []


def kernel(L0_attn_x, L0_attn_Wz, L0_attn_bz, L0_attn_Wh, L0_attn_bh,
           L0_ffn_w_freq_real, L0_ffn_w_freq_imag, L0_ffn_bias):
    x = np.ascontiguousarray(np.asarray(L0_attn_x, np.float32)).reshape(4 * 8192, D)
    wz = np.ascontiguousarray(np.asarray(L0_attn_Wz, np.float32))
    wh = np.ascontiguousarray(np.asarray(L0_attn_Wh, np.float32))
    wfr = np.ascontiguousarray(np.asarray(L0_ffn_w_freq_real, np.float32))
    wfi = np.ascontiguousarray(np.asarray(L0_ffn_w_freq_imag, np.float32))
    brows = np.ascontiguousarray(np.concatenate(
        [np.asarray(L0_attn_bz, np.float32),
         np.asarray(L0_attn_bh, np.float32),
         np.asarray(L0_ffn_bias, np.float32)]).reshape(1, 3 * D))

    k = np.arange(D, dtype=np.float64)
    ang = 2.0 * np.pi * np.outer(k, k) / D
    C16 = np.cos(ang).astype(np.float16)
    S16 = np.sin(ang).astype(np.float16)
    id16 = np.eye(128, dtype=np.float16)
    ones16 = np.ones((1, 512), np.float16)

    if not _NC_CACHE:
        _NC_CACHE.append(build_program())
    nc = _NC_CACHE[0]

    nonce = np.full((2, 1), np.random.randint(1, 2 ** 30), np.int32)
    in_maps = []
    for c in range(NCORES):
        role = c % 2
        in_maps.append({
            "L0_attn_x_shard": np.ascontiguousarray(x[TOK * c:TOK * (c + 1)]),
            "L0_attn_Wz": wz, "L0_attn_Wh": wh,
            "L0_ffn_w_freq_real": wfr, "L0_ffn_w_freq_imag": wfi,
            "brows": brows, "C16": C16, "S16": S16,
            "Crole": np.ascontiguousarray(C16[:, 512 * role:512 * (role + 1)]),
            "Srole": np.ascontiguousarray(S16[:, 512 * role:512 * (role + 1)]),
            "midx": np.ascontiguousarray(
                (np.arange(128, dtype=np.int32)[:, None]
                 + 512 * role + 128 * np.arange(4, dtype=np.int32)[None, :])),
            "hidx": (np.arange(128, dtype=np.int32) + 128 * role
                     ).reshape(128, 1),
            "fidx": np.array([[role, 2 + role], [8 + role, 8 + role]],
                             np.int32),
            "nonce": nonce,
            "rmask": np.full((128, 1), float(role), np.float32),
            "ident16": id16, "ones16": ones16,
        })
    res = run_bass_kernel_spmd(nc, in_maps, core_ids=list(range(NCORES)))
    out = np.concatenate([res.results[c]["out"] for c in range(NCORES)], axis=0)
    return out.reshape(4, 8192, D)


# version 17
# speedup vs baseline: 1.3218x; 1.3218x over previous
import numpy as np
import concourse.bass as bass
import concourse.mybir as mybir
from concourse.bass_utils import run_bass_kernel_spmd
from concourse.bass_core_rust import IndirectOffsetOnAxis

F32 = mybir.dt.float32
F16 = mybir.dt.float16
I32 = mybir.dt.int32
AF = mybir.ActivationFunctionType
ALU = mybir.AluOpType

NCORES = 8
D = 1024
TOK = 4096
NT = 32
NG = 8
KC = 8
NF = 8
NXS = 3
EPS = 1e-5

ENGS = ("sync", "tensor", "vector", "scalar", "gpsimd")


def semkey(engine, dma):
    return engine + ("_d" if dma else "_c")


SEMKEYS = ("sync_d", "tensor_c", "vector_c", "scalar_c", "scalar_d",
           "gpsimd_c", "gpsimd_d")


class Sched:
    def __init__(self):
        self.items = []
        self.counts = {k: 0 for k in SEMKEYS}

    def emit(self, engine, fn, reads=(), writes=(), tag="", dma=False,
             manual=False):
        key = semkey(engine, dma)
        inc = 16 if dma else 1
        self.counts[key] += inc
        ev = (key, self.counts[key], inc, manual, engine)
        self.items.append((engine, fn, tuple(reads), tuple(writes), tag, ev))
        return ev


def _ov(a, b):
    return a[0] == b[0] and a[1] < b[2] and b[1] < a[2]


def resolve(sched):
    last_write = {}
    readers = {}
    raw = []
    for engine, fn, reads, writes, tag, ev in sched.items:
        need = []
        for r in reads:
            for b, e in last_write.get(r[0], ()):
                if _ov(r, b):
                    need.append(e)
        for w in writes:
            for b, e in last_write.get(w[0], ()):
                if _ov(w, b):
                    need.append(e)
            for b, e in readers.get(w[0], ()):
                if _ov(w, b):
                    need.append(e)
        per = {}
        for (e_key, e_val, _inc, _m, e_eng) in need:
            per[e_key] = max(per.get(e_key, 0), e_val)
        raw.append(per)
        for w in writes:
            lw = last_write.setdefault(w[0], [])
            lw[:] = [(b, e) for b, e in lw
                     if not (w[1] <= b[1] and b[2] <= w[2])]
            lw.append((w, ev))
            rd = readers.setdefault(w[0], [])
            rd[:] = [(b, e) for b, e in rd
                     if not (w[1] <= b[1] and b[2] <= w[2])]
        for r in reads:
            readers.setdefault(r[0], []).append((r, ev))

    DRAIN_K = 8
    dkey = {e: e + "_d" for e in ("sync", "scalar", "gpsimd")}
    drains = {}
    cur = {k: 0 for k in SEMKEYS if k.endswith("_d")}
    since = {k: 0 for k in cur}
    bounds = {k: [] for k in cur}
    for idx, ((engine, fn, reads, writes, tag, ev), per) in enumerate(
            zip(sched.items, raw)):
        k = dkey.get(engine)
        if k is not None and (per or (ev[0] == k and since[k] >= DRAIN_K)):
            if since[k] > 0:
                drains.setdefault(idx, {})[k] = cur[k]
                bounds[k].append(cur[k])
                since[k] = 0
        if ev[0] in cur:
            cur[ev[0]] = ev[1]
            since[ev[0]] += 1
    for k in cur:
        if since[k] > 0:
            bounds[k].append(cur[k])
            drains.setdefault(len(sched.items), {})[k] = cur[k]

    import bisect
    def round_up(k, v):
        bl = bounds.get(k)
        if not bl:
            return v
        i = bisect.bisect_left(bl, v)
        return bl[i] if i < len(bl) else bl[-1]

    out = []
    for per in raw:
        rounded = {}
        for k, v in per.items():
            rounded[k] = round_up(k, v) if k.endswith("_d") else v
        out.append(rounded)
    return out, drains


def buf(name, lo=0, hi=1 << 30):
    return (name, lo, hi)


def build_program(detect_races=False, audit=False):
    nc = bass.Bass("TRN2", target_bir_lowering=False,
                   detect_race_conditions=detect_races)

    x_h = nc.dram_tensor("L0_attn_x_shard", [TOK, D], F32, kind="ExternalInput").ap()
    wz_h = nc.dram_tensor("L0_attn_Wz", [D, D], F32, kind="ExternalInput").ap()
    wh_h = nc.dram_tensor("L0_attn_Wh", [D, D], F32, kind="ExternalInput").ap()
    wfr_h = nc.dram_tensor("L0_ffn_w_freq_real", [D, D], F32, kind="ExternalInput").ap()
    wfi_h = nc.dram_tensor("L0_ffn_w_freq_imag", [D, D], F32, kind="ExternalInput").ap()
    brows_h = nc.dram_tensor("brows", [1, 3 * D], F32, kind="ExternalInput").ap()
    c16_h = nc.dram_tensor("C16", [D, D], F16, kind="ExternalInput").ap()
    s16_h = nc.dram_tensor("S16", [D, D], F16, kind="ExternalInput").ap()
    id16_h = nc.dram_tensor("ident16", [128, 128], F16, kind="ExternalInput").ap()
    ones16_h = nc.dram_tensor("ones16", [1, 512], F16, kind="ExternalInput").ap()
    crole_h = nc.dram_tensor("Crole", [D, 512], F16, kind="ExternalInput").ap()
    srole_h = nc.dram_tensor("Srole", [D, 512], F16, kind="ExternalInput").ap()
    midx_h = nc.dram_tensor("midx", [128, 4], I32, kind="ExternalInput").ap()
    hidx_h = nc.dram_tensor("hidx", [128, 1], I32, kind="ExternalInput").ap()
    fidx_h = nc.dram_tensor("fidx", [2, 2], I32, kind="ExternalInput").ap()
    nonce_h = nc.dram_tensor("nonce", [2, 1], I32, kind="ExternalInput").ap()
    rmask_h = nc.dram_tensor("rmask", [128, 1], F32, kind="ExternalInput").ap()
    out_h = nc.dram_tensor("out", [TOK, D], F32, kind="ExternalOutput").ap()

    sh_m = nc.dram_tensor("sh_m", [D, D], F16, addr_space="Shared").ap()
    sh_h = nc.dram_tensor("sh_h", [256, 8], F32, addr_space="Shared").ap()
    sh_flag = nc.dram_tensor("sh_flag", [16, 1], I32, addr_space="Shared").ap()

    A = {}

    def sb(name, shape, dtype):
        A[name] = nc.alloc_sbuf_tensor(name, shape, dtype).ap()
        return A[name]

    sb("sb_ident16", [128, 128], F16)
    sb("sb_brows", [1, 3 * D], F32)
    sb("brows16", [1, 3 * D], F16)
    sb("sb_ones16", [1, 512], F16)
    sb("ones_i32", [1, 8], I32)
    sb("flag_rd", [1, 8], I32)
    sb("wz16", [128, KC * D], F16)
    sb("wh16", [128, KC * D], F16)
    ar = nc.alloc_sbuf_tensor("arena", [128, 4 * 8192], F16).ap()
    WFR16 = ar[:, 0:8192]
    WFI16 = ar[:, 8192:16384]
    C16 = ar[:, 0:8192]
    S16 = ar[:, 8192:16384]
    CROLE = ar[:, 16384:20480]
    SROLE = ar[:, 20480:24576]
    M16 = ar[:, 16384:24576]
    PT2 = ar[:, 24576:28672]
    T2T2 = ar[:, 28672:32768]
    G16 = ar[:, 24576:32768]
    sb("wstage", [128, 2 * D], F32)
    sb("xpool", [128, NXS * D], F32)
    sb("xasm", [128, 2 * D], F32)
    sb("xn", [128, 2 * D], F16)
    sb("xnT", [128, 2 * 4096], F16)
    sb("pre", [128, 2 * NF * 512], F16)
    sb("sumc", [128, NT], F32)
    sb("ssqc", [128, NT], F32)
    sb("meanc", [128, NT], F32)
    sb("varc", [128, NT], F32)
    sb("stdc", [128, NT], F32)
    sb("rstdc", [128, NT], F32)
    sb("zb", [128, 512], F32)
    sb("ab", [128, 2 * 512], F32)
    sb("bb", [128, 2 * 512], F32)
    sb("hb", [128, 2 * 512], F32)
    sb("t0b", [128, 2 * 512], F16)
    sb("zeros", [128, 512], F16)
    sb("hcarry", [128, 8], F32)
    sb("sb_midx", [128, 4], I32)
    sb("sb_hidx", [128, 1], I32)
    sb("sb_fidx", [2, 2], I32)
    sb("sb_nonce", [2, 1], I32)
    sb("sb_rmask", [128, 1], F32)
    sb("flag_rd2", [1, 16], I32)
    sb("gcarry", [128, 8], F32)
    sb("hin", [128, 8], F32)
    sb("tiny", [128, 8], F32)
    sb("outst", [128, 2 * D], F32)
    sb("mtmp", [128, D], F16)

    JUNK = A["outst"][:, 0:D]

    PS = {}

    def ps(name, shape, dtype):
        PS[name] = nc.alloc_psum_tensor(name, shape, dtype).ap()
        return PS[name]

    ps("uB0", [128, 512], F32)
    ps("uB1", [128, 512], F32)
    ps("hB0", [128, 512], F32)
    ps("hB1", [128, 512], F32)
    ps("yB0", [128, 512], F32)
    ps("yB1", [128, 512], F32)
    ps("trA", [128, 1024], F16)
    ps("trB", [128, 1024], F16)

    POLLSEM = [None]
    sched = Sched()
    E = sched.emit

    def xslot(i):
        return A["xpool"][:, (i % NXS) * D:((i % NXS) + 1) * D]

    def xbuf(i):
        return buf("xpool", (i % NXS) * D, ((i % NXS) + 1) * D)

    def xaslot(i):
        return A["xasm"][:, (i % 2) * D:((i % 2) + 1) * D]

    def xabuf(i):
        return buf("xasm", (i % 2) * D, ((i % 2) + 1) * D)

    PTv = PT2
    T2v = T2T2
    T1v = A["pre"][:, 0:4096]

    def ld(dst, src, wb, tag, eng="sync"):
        E(eng, lambda e, s_, d=dst, s=src: e.dma_start(out=d, in_=s),
          writes=(wb,), tag=tag, dma=True)

    ld(A["sb_ident16"], id16_h, buf("sb_ident16"), "ld_id")
    ld(A["sb_brows"], brows_h, buf("sb_brows"), "ld_br")
    ld(A["sb_ones16"], ones16_h, buf("sb_ones16"), "ld_on")

    E("gpsimd", lambda e, s_: e.memset(A["zeros"], 0.0), writes=(buf("zeros"),), tag="mz")
    E("gpsimd", lambda e, s_: e.memset(A["hcarry"], 0.0), writes=(buf("hcarry"),), tag="mh")
    E("gpsimd", lambda e, s_: e.memset(A["gcarry"], 1.0), writes=(buf("gcarry"),), tag="mg")
    E("gpsimd", lambda e, s_: e.memset(A["ones_i32"], 1), writes=(buf("ones_i32"),), tag="mo")
    E("vector", lambda e, s_: e.tensor_copy(A["brows16"], A["sb_brows"]),
      reads=(buf("sb_brows"),), writes=(buf("brows16"),), tag="cbr")

    def cast_weight(h_ap, dst16, region, base, tag, q="scalar"):
        sl = 0 if q == "scalar" else 1
        stg = A["wstage"][:, sl * D:(sl + 1) * D]
        stb = buf("wstage", sl * D, (sl + 1) * D)
        for k in range(KC):
            E(q, lambda e, s_, k=k, h=h_ap, stg=stg: e.dma_start(
                out=stg, in_=h[128 * k:128 * (k + 1), :]),
              writes=(stb,), tag=f"{tag}l{k}", dma=True)
            E("vector", lambda e, s_, k=k, d=dst16, stg=stg: e.tensor_copy(
                d[:, k * D:(k + 1) * D], stg),
              reads=(stb,),
              writes=(buf(region, base + k * D, base + (k + 1) * D),),
              tag=f"{tag}c{k}")

    for k in range(KC):
        ld(CROLE[:, k * 512:(k + 1) * 512], crole_h[128 * k:128 * (k + 1), :],
           buf("arena", 16384 + k * 512, 16384 + (k + 1) * 512), f"lCr{k}",
           eng="scalar")
        ld(SROLE[:, k * 512:(k + 1) * 512], srole_h[128 * k:128 * (k + 1), :],
           buf("arena", 20480 + k * 512, 20480 + (k + 1) * 512), f"lSr{k}",
           eng="gpsimd")
    cast_weight(wfr_h, WFR16, "arena", 0, "wr", q="scalar")
    cast_weight(wfi_h, WFI16, "arena", 8192, "wi", q="gpsimd")
    cast_weight(wz_h, A["wz16"], "wz16", 0, "wz", q="scalar")
    cast_weight(wh_h, A["wh16"], "wh16", 0, "wh", q="gpsimd")
    for dst, src, wb, tg in ((A["sb_midx"], midx_h, buf("sb_midx"), "lmi"),
                             (A["sb_hidx"], hidx_h, buf("sb_hidx"), "lhi"),
                             (A["sb_fidx"], fidx_h, buf("sb_fidx"), "lfi"),
                             (A["sb_nonce"], nonce_h, buf("sb_nonce"), "lno"),
                             (A["sb_rmask"], rmask_h, buf("sb_rmask"), "lrm")):
        ld(dst, src, wb, tg)

    for i in range(NT):
        ld(xslot(i), x_h[128 * i:128 * (i + 1), :], xbuf(i), f"px{i}")
        E("scalar", lambda e, s_, i=i: e.activation(
            JUNK, xslot(i), AF.Identity, bias=0.0, scale=1.0,
            accum_out=A["sumc"][:, i:i + 1]),
          reads=(xbuf(i),), writes=(buf("sumc", i, i + 1), buf("outst", 0, D)),
          tag=f"sum{i}")
        E("scalar", lambda e, s_, i=i: e.activation(
            JUNK, xslot(i), AF.Square, bias=0.0, scale=1.0,
            accum_out=A["ssqc"][:, i:i + 1]),
          reads=(xbuf(i),), writes=(buf("ssqc", i, i + 1), buf("outst", 0, D)),
          tag=f"ssq{i}")

    E("vector", lambda e, s_: e.tensor_scalar(
        A["meanc"], A["sumc"], 1.0 / D, None, ALU.mult),
      reads=(buf("sumc"),), writes=(buf("meanc"),), tag="st1")
    E("vector", lambda e, s_: e.tensor_tensor(
        A["stdc"], A["meanc"], A["meanc"], ALU.mult),
      reads=(buf("meanc"),), writes=(buf("stdc"),), tag="st2")
    E("vector", lambda e, s_: e.tensor_scalar(
        A["varc"], A["ssqc"], 1.0 / D, None, ALU.mult),
      reads=(buf("ssqc"),), writes=(buf("varc"),), tag="st3")
    E("vector", lambda e, s_: e.tensor_tensor(
        A["varc"], A["varc"], A["stdc"], ALU.subtract),
      reads=(buf("varc"), buf("stdc")), writes=(buf("varc"),), tag="st4")
    E("vector", lambda e, s_: e.tensor_scalar(
        A["varc"], A["varc"], EPS, None, ALU.add),
      reads=(buf("varc"),), writes=(buf("varc"),), tag="st5")
    E("scalar", lambda e, s_: e.activation(A["stdc"], A["varc"], AF.Sqrt),
      reads=(buf("varc"),), writes=(buf("stdc"),), tag="sqrt")
    E("vector", lambda e, s_: e.reciprocal(A["rstdc"], A["stdc"]),
      reads=(buf("stdc"),), writes=(buf("rstdc"),), tag="rstd")

    slot0 = (buf("arena", 0, 8192),)
    slot1 = (buf("arena", 8192, 16384),)
    slot2 = (buf("arena", 16384, 24576),)
    slot3 = (buf("arena", 24576, 32768),)

    for oc in range(KC):
        pt_bank = PS["yB0"] if oc % 2 == 0 else PS["yB1"]
        t1_bank = PS["uB0"] if oc % 2 == 0 else PS["uB1"]
        t2_bank = PS["hB0"] if oc % 2 == 0 else PS["hB1"]
        pt_bb = buf("yB0" if oc % 2 == 0 else "yB1")
        t1_bb = buf("uB0" if oc % 2 == 0 else "uB1")
        t2_bb = buf("hB0" if oc % 2 == 0 else "hB1")

        def s1(e, s_, oc=oc, pt=pt_bank, t1=t1_bank, t2=t2_bank):
            for k in range(KC):
                e.matmul(pt, WFR16[:, k * D + 128 * oc:k * D + 128 * (oc + 1)],
                         CROLE[:, k * 512:(k + 1) * 512],
                         start=(k == 0), stop=False)
            for k in range(KC):
                e.matmul(pt, WFI16[:, k * D + 128 * oc:k * D + 128 * (oc + 1)],
                         SROLE[:, k * 512:(k + 1) * 512],
                         start=False, stop=(k == KC - 1))
            for k in range(KC):
                e.matmul(t1, WFI16[:, k * D + 128 * oc:k * D + 128 * (oc + 1)],
                         CROLE[:, k * 512:(k + 1) * 512],
                         start=(k == 0), stop=(k == KC - 1))
            last = None
            for k in range(KC):
                last = e.matmul(
                    t2, WFR16[:, k * D + 128 * oc:k * D + 128 * (oc + 1)],
                    SROLE[:, k * 512:(k + 1) * 512],
                    start=(k == 0), stop=(k == KC - 1))
            return last
        E("tensor", s1, reads=slot0 + slot1 + slot2,
          writes=(pt_bb, t1_bb, t2_bb), tag=f"ms1_{oc}")
        E("scalar", lambda e, s_, oc=oc, pt=pt_bank: e.activation(
            PTv[:, oc * 512:(oc + 1) * 512], pt, AF.Identity),
          reads=(pt_bb,),
          writes=(buf("arena", 24576 + oc * 512, 24576 + (oc + 1) * 512),),
          tag=f"ePT{oc}")
        E("scalar", lambda e, s_, oc=oc, t1=t1_bank: e.activation(
            T1v[:, oc * 512:(oc + 1) * 512], t1, AF.Identity, scale=-1.0),
          reads=(t1_bb,), writes=(buf("pre", oc * 512, (oc + 1) * 512),),
          tag=f"eT1{oc}")
        E("scalar", lambda e, s_, oc=oc, t2=t2_bank: e.activation(
            T2v[:, oc * 512:(oc + 1) * 512], t2, AF.Identity),
          reads=(t2_bb,),
          writes=(buf("arena", 28672 + oc * 512, 28672 + (oc + 1) * 512),),
          tag=f"eT2{oc}")

    for k in range(KC):
        ld(C16[:, k * D:(k + 1) * D], c16_h[128 * k:128 * (k + 1), :],
           buf("arena", k * D, (k + 1) * D), f"lC{k}", eng="scalar")
        ld(S16[:, k * D:(k + 1) * D], s16_h[128 * k:128 * (k + 1), :],
           buf("arena", 8192 + k * D, 8192 + (k + 1) * D), f"lS{k}",
           eng="scalar")

    for rc in range(4):
        bank = PS["yB0"] if rc % 2 == 0 else PS["yB1"]
        bb = buf("yB0" if rc % 2 == 0 else "yB1")
        for half in range(2):
            def s2(e, s_, rc=rc, half=half, bank=bank):
                hs = 512 * half
                last = None
                for (lv, rv) in ((PTv, C16), (T2v, S16), (T1v, S16)):
                    for oc in range(KC):
                        last = e.matmul(
                            bank,
                            lv[:, oc * 512 + 128 * rc:oc * 512 + 128 * (rc + 1)],
                            rv[:, oc * D + hs:oc * D + hs + 512],
                            start=(lv is PTv and oc == 0),
                            stop=(lv is T1v and oc == KC - 1))
                return last
            E("tensor", s2,
              reads=slot0 + slot1 + slot3 + (buf("pre", 0, 4096),),
              writes=(bb,), tag=f"ms2_{rc}_{half}")
            E("scalar", lambda e, s_, half=half, bank=bank: e.activation(
                A["mtmp"][:, half * 512:(half + 1) * 512], bank, AF.Identity,
                scale=1.0 / D),
              reads=(bb,), writes=(buf("mtmp", half * 512, (half + 1) * 512),),
              tag=f"eM{rc}_{half}")

        E("gpsimd", lambda e, s_, rc=rc: e.indirect_dma_start(
            sh_m, IndirectOffsetOnAxis(A["sb_midx"][:, rc:rc + 1], 0),
            A["mtmp"], None),
          reads=(buf("mtmp", 0, D), buf("sb_midx", rc, rc + 1)),
          writes=(buf("shm", 128 * rc, 128 * (rc + 1)),), tag=f"wM{rc}",
          dma=True)

    def m_flag_poll(e, s_):
        e.indirect_dma_start(
            sh_flag, IndirectOffsetOnAxis(A["sb_fidx"][:, 0:1], 0),
            A["sb_nonce"], None).then_inc(POLLSEM[0], 16)
        with e.register("m_rgo") as rgo, e.register("m_rn") as rn, \
             e.register("m_r0") as r0, e.register("m_r1") as r1:
            e.reg_load(rn, A["sb_nonce"][0:1, 0:1])
            e.reg_mov(rgo, 1)
            with e.While(rgo):
                for _d in range(24):
                    e.tensor_copy(A["wstage"][:, 0:128],
                                  A["zeros"][:, 0:128])
                e.dma_start(out=A["flag_rd2"],
                            in_=sh_flag).then_inc(POLLSEM[0], 16)
                e.nop()
                e.reg_load(r0, A["flag_rd2"][0:1, 0:1])
                e.reg_load(r1, A["flag_rd2"][0:1, 1:2])
                e.reg_alu(r0, r0, rn, ALU.is_equal)
                e.reg_alu(r1, r1, rn, ALU.is_equal)
                e.reg_alu(r0, r0, r1, ALU.add)
                e.reg_alu(rgo, r0, 2, ALU.is_lt)
        return e.nop()
    E("gpsimd", m_flag_poll,
      reads=(buf("shm", 0, 512), buf("sb_nonce"), buf("sb_fidx", 0, 1)),
      writes=(buf("flag_rd2"),), tag="mpoll")
    for k in range(KC):
        E("gpsimd", lambda e, s_, k=k: e.dma_start(
            out=M16[:, k * D:(k + 1) * D],
            in_=sh_m[128 * k:128 * (k + 1), :]),
          reads=(buf("flag_rd2"),),
          writes=(buf("arena", 16384 + k * D, 16384 + (k + 1) * D),),
          tag=f"ldM{k}", dma=True)

    def emit_tile_front(i):
        g, t = i // 4, i % 4
        gs = (g % 2) * 4096
        ld(xslot(i), x_h[128 * i:128 * (i + 1), :], xbuf(i), f"mx{i}")
        xnv = A["xn"][:, (i % 2) * D:((i % 2) + 1) * D]
        xnb = buf("xn", (i % 2) * D, ((i % 2) + 1) * D)
        E("vector", lambda e, s_, i=i, xnv=xnv: e.tensor_scalar(
            xnv, xslot(i), A["meanc"][:, i:i + 1], A["rstdc"][:, i:i + 1],
            ALU.subtract, ALU.mult),
          reads=(xbuf(i), buf("meanc", i, i + 1), buf("rstdc", i, i + 1)),
          writes=(xnb,), tag=f"xn{i}")

        def trx(e, s_, xnv=xnv):
            last = None
            for k in range(KC):
                last = e.transpose(PS["trA"][:, 128 * k:128 * (k + 1)],
                                   xnv[:, 128 * k:128 * (k + 1)], A["sb_ident16"])
            return last
        E("tensor", trx, reads=(xnb, buf("sb_ident16")),
          writes=(buf("trA"),), tag=f"trx{i}")

        def exn(e, s_, gs=gs, t=t):
            out3 = A["xnT"][:, gs:gs + 4096].rearrange(
                "p (k q) -> p k q", k=KC)[:, :, t * 128:(t + 1) * 128]
            in3 = PS["trA"].rearrange("p (k q) -> p k q", k=KC)
            return e.copy(out3, in3)
        E("scalar", exn, reads=(buf("trA"),),
          writes=tuple(buf("xnT", gs + k * 512 + t * 128,
                           gs + k * 512 + (t + 1) * 128) for k in range(KC)),
          tag=f"exn{i}")

    def emit_group_f(g, f):
        gs = (g % 2) * 4096
        xnT_read = buf("xnT", gs, gs + 4096)
        ub = PS["uB0"] if f % 2 == 0 else PS["uB1"]
        hbk = PS["hB0"] if f % 2 == 0 else PS["hB1"]
        yb = PS["yB0"] if f % 2 == 0 else PS["yB1"]
        ubb = buf("uB0" if f % 2 == 0 else "uB1")
        hbb = buf("hB0" if f % 2 == 0 else "hB1")
        ybb = buf("yB0" if f % 2 == 0 else "yB1")
        fs = (f % 2) * 512

        def mmproj(e, w16, bias_off, bank, gs=gs, f=f):
            e.matmul(bank,
                     A["brows16"][:, bias_off + f * 128:bias_off + (f + 1) * 128],
                     A["sb_ones16"], start=True, stop=False)
            last = None
            for k in range(KC):
                last = e.matmul(
                    bank, w16[:, k * D + 128 * f:k * D + 128 * (f + 1)],
                    A["xnT"][:, gs + k * 512:gs + (k + 1) * 512],
                    start=False, stop=(k == KC - 1))
            return last

        E("tensor", lambda e, s_: mmproj(e, A["wz16"], 0, ub),
          reads=(xnT_read, buf("wz16"), buf("brows16"), buf("sb_ones16")),
          writes=(ubb,), tag=f"u{g}_{f}")
        E("scalar", lambda e, s_: e.activation(A["zb"], ub, AF.Sigmoid),
          reads=(ubb,), writes=(buf("zb"),), tag=f"z{g}_{f}")
        E("scalar", lambda e, s_: e.activation(
            A["ab"][:, fs:fs + 512], ub, AF.Sigmoid, scale=-1.0),
          reads=(ubb,), writes=(buf("ab", fs, fs + 512),), tag=f"a{g}_{f}")
        E("tensor", lambda e, s_: mmproj(e, A["wh16"], D, hbk),
          reads=(xnT_read, buf("wh16"), buf("brows16"), buf("sb_ones16")),
          writes=(hbb,), tag=f"h{g}_{f}")
        E("vector", lambda e, s_: e.tensor_tensor(
            A["bb"][:, fs:fs + 512], A["zb"], hbk, ALU.mult),
          reads=(buf("zb"), hbb), writes=(buf("bb", fs, fs + 512),),
          tag=f"b{g}_{f}")

        E("vector", lambda e, s_, f=f, fs=fs: e.tensor_tensor_scan(
            A["hb"][:, fs:fs + 512], A["ab"][:, fs:fs + 512],
            A["bb"][:, fs:fs + 512], A["hcarry"][:, f:f + 1],
            ALU.mult, ALU.add),
          reads=(buf("ab", fs, fs + 512), buf("bb", fs, fs + 512),
                 buf("hcarry", f, f + 1)),
          writes=(buf("hb", fs, fs + 512),), tag=f"sh{g}_{f}")
        E("vector", lambda e, s_, f=f, fs=fs: e.tensor_copy(
            A["hcarry"][:, f:f + 1], A["hb"][:, fs + 511:fs + 512]),
          reads=(buf("hb", fs, fs + 512),),
          writes=(buf("hcarry", f, f + 1),), tag=f"shc{g}_{f}")

        if g < 2:
            goff = (g * 8 + f) * 512

            E("vector", lambda e, s_, f=f, fs=fs, goff=goff: e.tensor_tensor_scan(
                G16[:, goff:goff + 512], A["ab"][:, fs:fs + 512],
                A["zeros"], A["gcarry"][:, f:f + 1], ALU.mult, ALU.add),
              reads=(buf("ab", fs, fs + 512), buf("zeros"),
                     buf("gcarry", f, f + 1)),
              writes=(buf("arena", 24576 + goff, 24576 + goff + 512),),
              tag=f"sg{g}_{f}")
            E("vector", lambda e, s_, f=f, goff=goff: e.tensor_copy(
                A["gcarry"][:, f:f + 1], G16[:, goff + 511:goff + 512]),
              reads=(buf("arena", 24576 + goff, 24576 + goff + 512),),
              writes=(buf("gcarry", f, f + 1),), tag=f"sgc{g}_{f}")

        E("tensor", lambda e, s_: mmproj(e, M16, 2 * D, yb),
          reads=(xnT_read, buf("arena", 16384, 24576), buf("brows16"),
                 buf("sb_ones16")),
          writes=(ybb,), tag=f"y{g}_{f}")
        E("scalar", lambda e, s_: e.activation(
            A["t0b"][:, fs:fs + 512], yb, AF.Identity),
          reads=(ybb,), writes=(buf("t0b", fs, fs + 512),), tag=f"t0{g}_{f}")
        poff = (g % 2) * 4096 + f * 512
        E("gpsimd", lambda e, s_, poff=poff, fs=fs: e.tensor_tensor(
            A["pre"][:, poff:poff + 512], A["t0b"][:, fs:fs + 512],
            A["hb"][:, fs:fs + 512], ALU.add),
          reads=(buf("t0b", fs, fs + 512), buf("hb", fs, fs + 512)),
          writes=(buf("pre", poff, poff + 512),), tag=f"pre{g}_{f}")

    def emit_tile_back(i):
        g, t = i // 4, i % 4
        poffg = (g % 2) * 4096
        ld(xaslot(i), x_h[128 * i:128 * (i + 1), :], xabuf(i), f"ax{i}")

        def trp(e, s_, t=t, poffg=poffg):
            last = None
            for f in range(NF):
                last = e.transpose(
                    PS["trB"][:, 128 * f:128 * (f + 1)],
                    A["pre"][:, poffg + f * 512 + t * 128:
                             poffg + f * 512 + (t + 1) * 128],
                    A["sb_ident16"])
            return last
        E("tensor", trp,
          reads=(buf("pre", poffg, poffg + 4096), buf("sb_ident16")),
          writes=(buf("trB"),), tag=f"trp{i}")
        osl = (i % 2) * D
        E("vector", lambda e, s_, i=i, osl=osl: e.tensor_tensor(
            A["outst"][:, osl:osl + D], xaslot(i), PS["trB"], ALU.add),
          reads=(xabuf(i), buf("trB")),
          writes=(buf("outst", osl, osl + D),), tag=f"oadd{i}")
        E("scalar", lambda e, s_, i=i, osl=osl: e.dma_start(
            out=out_h[128 * i:128 * (i + 1), :],
            in_=A["outst"][:, osl:osl + D]),
          reads=(buf("outst", osl, osl + D),),
          writes=(buf("outhbm", 128 * i, 128 * (i + 1)),), tag=f"st{i}",
          dma=True)

    for g in range(NG):
        for t in range(4):
            emit_tile_front(4 * g + t)
        for f in range(NF):
            emit_group_f(g, f)
        if g >= 1:
            for t in range(4):
                emit_tile_back(4 * (g - 1) + t)
    for t in range(4):
        emit_tile_back(4 * 7 + t)

    E("gpsimd", lambda e, s_: e.indirect_dma_start(
        sh_h, IndirectOffsetOnAxis(A["sb_hidx"], 0), A["hcarry"], None),
      reads=(buf("hcarry", 0, 8), buf("sb_hidx")),
      writes=(buf("shh"),), tag="hxw", dma=True)

    def h_flag_poll(e, s_):
        e.indirect_dma_start(
            sh_flag, IndirectOffsetOnAxis(A["sb_fidx"][:, 1:2], 0),
            A["sb_nonce"], None).then_inc(POLLSEM[0], 16)
        with e.register("h_rgo") as rgo, e.register("h_rn") as rn, \
             e.register("h_r0") as r0, e.register("h_r1") as r1:
            e.reg_load(rn, A["sb_nonce"][0:1, 0:1])
            e.reg_mov(rgo, 1)
            with e.While(rgo):
                for _d in range(24):
                    e.tensor_copy(A["wstage"][:, 0:128],
                                  A["zeros"][:, 0:128])
                e.dma_start(out=A["flag_rd2"],
                            in_=sh_flag).then_inc(POLLSEM[0], 16)
                e.nop()
                e.reg_load(r0, A["flag_rd2"][0:1, 2:3])
                e.reg_load(r1, A["flag_rd2"][0:1, 3:4])
                e.reg_alu(r0, r0, rn, ALU.is_equal)
                e.reg_alu(r1, r1, rn, ALU.is_equal)
                e.reg_alu(r0, r0, r1, ALU.add)
                e.reg_alu(rgo, r0, 2, ALU.is_lt)
        return e.nop()
    E("gpsimd", h_flag_poll,
      reads=(buf("shh"), buf("sb_nonce"), buf("sb_fidx", 1, 2)),
      writes=(buf("flag_rd2"),), tag="hpoll")
    E("gpsimd", lambda e, s_: e.dma_start(out=A["tiny"], in_=sh_h[0:128, :]),
      reads=(buf("flag_rd2"),), writes=(buf("tiny"),), tag="hrd", dma=True)
    E("vector", lambda e, s_: e.tensor_scalar_mul(
        A["hin"], A["tiny"], A["sb_rmask"]),
      reads=(buf("tiny"), buf("sb_rmask")), writes=(buf("hin"),), tag="hmask")

    CORR = A["mtmp"]
    for i in range(8):
        g, t = i // 4, i % 4
        E("gpsimd", lambda e, s_, i=i: e.dma_start(
            out=xaslot(i), in_=out_h[128 * i:128 * (i + 1), :]),
          reads=(buf("outhbm", 128 * i, 128 * (i + 1)),),
          writes=(xabuf(i),), tag=f"rb{i}", dma=True)

        def mkcorr(e, s_, g=g, t=t):
            last = None
            for f in range(NF):
                goff = (g * 8 + f) * 512 + t * 128
                last = e.tensor_scalar_mul(
                    CORR[:, f * 128:(f + 1) * 128],
                    G16[:, goff:goff + 128], A["hin"][:, f:f + 1])
            return last
        E("vector", mkcorr, reads=(buf("arena", 24576, 32768), buf("hin")),
          writes=(buf("mtmp", 0, D),), tag=f"corr{i}")

        def trc(e, s_):
            last = None
            for f in range(NF):
                last = e.transpose(PS["trB"][:, 128 * f:128 * (f + 1)],
                                   CORR[:, f * 128:(f + 1) * 128],
                                   A["sb_ident16"])
            return last
        E("tensor", trc, reads=(buf("mtmp", 0, D), buf("sb_ident16")),
          writes=(buf("trB"),), tag=f"trc{i}")
        osl = (i % 2) * D
        E("vector", lambda e, s_, i=i, osl=osl: e.tensor_tensor(
            A["outst"][:, osl:osl + D], xaslot(i), PS["trB"], ALU.add),
          reads=(xabuf(i), buf("trB")),
          writes=(buf("outst", osl, osl + D),), tag=f"o3_{i}")
        E("scalar", lambda e, s_, i=i, osl=osl: e.dma_start(
            out=out_h[128 * i:128 * (i + 1), :],
            in_=A["outst"][:, osl:osl + D]),
          reads=(buf("outst", osl, osl + D),),
          writes=(buf("outhbm", 128 * i, 128 * (i + 1)),), tag=f"s3_{i}",
          dma=True)

    waits, drains = resolve(sched)
    per_engine = {e: [] for e in ENGS}
    per_engine_idx = {e: [] for e in ENGS}
    for gi, (item, w) in enumerate(zip(sched.items, waits)):
        per_engine[item[0]].append((item, w))
        per_engine_idx[item[0]].append(gi)

    from contextlib import ExitStack
    with ExitStack() as stack:
        sems = {k: stack.enter_context(nc.semaphore(f"sem_{k}")) for k in SEMKEYS}
        sems["poll"] = stack.enter_context(nc.semaphore("sem_poll"))
        POLLSEM[0] = sems["poll"]
        block = stack.enter_context(nc.Block())

        def emit_stream(eng_name, eng):
            hw = {k: 0 for k in SEMKEYS}
            for gi, ((engine, fn, reads, writes, tag, ev), w) in zip(
                    per_engine_idx[eng_name], per_engine[eng_name]):
                dr = drains.get(gi, {})
                for dkey2, dval in sorted(dr.items()):
                    if dval > hw[dkey2]:
                        eng.wait_ge(sems[dkey2], dval)
                        hw[dkey2] = dval
                for wkey, wval in sorted(w.items()):
                    if wval > hw[wkey]:
                        eng.wait_ge(sems[wkey], wval)
                        hw[wkey] = wval
                last = fn(eng, sems[ev[0]])
                if not ev[3]:
                    assert last is not None, f"item {tag} returned None"
                    last.then_inc(sems[ev[0]], ev[2])
            if eng_name == "gpsimd":
                eng.wait_ge(sems["scalar_c"], sched.counts["scalar_c"])
                eng.wait_ge(sems["scalar_d"], sched.counts["scalar_d"])
                eng.wait_ge(sems["sync_d"], sched.counts["sync_d"])

        @block.sync
        def _(sync):
            emit_stream("sync", sync)

        @block.tensor
        def _(tensor):
            emit_stream("tensor", tensor)

        @block.vector
        def _(vector):
            emit_stream("vector", vector)

        @block.scalar
        def _(scalar):
            emit_stream("scalar", scalar)

        @block.gpsimd
        def _(gpsimd):
            emit_stream("gpsimd", gpsimd)

    return nc


_NC_CACHE = []


def kernel(L0_attn_x, L0_attn_Wz, L0_attn_bz, L0_attn_Wh, L0_attn_bh,
           L0_ffn_w_freq_real, L0_ffn_w_freq_imag, L0_ffn_bias):
    x = np.ascontiguousarray(np.asarray(L0_attn_x, np.float32)).reshape(4 * 8192, D)
    wz = np.ascontiguousarray(np.asarray(L0_attn_Wz, np.float32))
    wh = np.ascontiguousarray(np.asarray(L0_attn_Wh, np.float32))
    wfr = np.ascontiguousarray(np.asarray(L0_ffn_w_freq_real, np.float32))
    wfi = np.ascontiguousarray(np.asarray(L0_ffn_w_freq_imag, np.float32))
    brows = np.ascontiguousarray(np.concatenate(
        [np.asarray(L0_attn_bz, np.float32),
         np.asarray(L0_attn_bh, np.float32),
         np.asarray(L0_ffn_bias, np.float32)]).reshape(1, 3 * D))

    k = np.arange(D, dtype=np.float64)
    ang = 2.0 * np.pi * np.outer(k, k) / D
    C16 = np.cos(ang).astype(np.float16)
    S16 = np.sin(ang).astype(np.float16)
    id16 = np.eye(128, dtype=np.float16)
    ones16 = np.ones((1, 512), np.float16)

    if not _NC_CACHE:
        _NC_CACHE.append(build_program())
    nc = _NC_CACHE[0]

    nonce = np.full((2, 1), np.random.randint(1, 2 ** 30), np.int32)
    in_maps = []
    for c in range(NCORES):
        role = c % 2
        in_maps.append({
            "L0_attn_x_shard": np.ascontiguousarray(x[TOK * c:TOK * (c + 1)]),
            "L0_attn_Wz": wz, "L0_attn_Wh": wh,
            "L0_ffn_w_freq_real": wfr, "L0_ffn_w_freq_imag": wfi,
            "brows": brows, "C16": C16, "S16": S16,
            "Crole": np.ascontiguousarray(C16[:, 512 * role:512 * (role + 1)]),
            "Srole": np.ascontiguousarray(S16[:, 512 * role:512 * (role + 1)]),
            "midx": np.ascontiguousarray(
                (np.arange(128, dtype=np.int32)[:, None]
                 + 512 * role + 128 * np.arange(4, dtype=np.int32)[None, :])),
            "hidx": (np.arange(128, dtype=np.int32) + 128 * role
                     ).reshape(128, 1),
            "fidx": np.array([[role, 2 + role], [8 + role, 8 + role]],
                             np.int32),
            "nonce": nonce,
            "rmask": np.full((128, 1), float(role), np.float32),
            "ident16": id16, "ones16": ones16,
        })
    res = run_bass_kernel_spmd(nc, in_maps, core_ids=list(range(NCORES)))
    out = np.concatenate([res.results[c]["out"] for c in range(NCORES)], axis=0)
    return out.reshape(4, 8192, D)


# version 18
# speedup vs baseline: 1.9533x; 1.4777x over previous
import numpy as np
import concourse.bass as bass
import concourse.mybir as mybir
from concourse.bass_utils import run_bass_kernel_spmd
from concourse.bass_core_rust import IndirectOffsetOnAxis

F32 = mybir.dt.float32
F16 = mybir.dt.float16
I32 = mybir.dt.int32
AF = mybir.ActivationFunctionType
ALU = mybir.AluOpType

NCORES = 8
D = 1024
TOK = 4096
NT = 32
NG = 8
KC = 8
NF = 8
NXS = 3
EPS = 1e-5

ENGS = ("sync", "tensor", "vector", "scalar", "gpsimd")


def semkey(engine, dma):
    return engine + ("_d" if dma else "_c")


SEMKEYS = ("sync_d", "tensor_c", "vector_c", "scalar_c", "scalar_d",
           "gpsimd_c", "gpsimd_d")


class Sched:
    def __init__(self):
        self.items = []
        self.counts = {k: 0 for k in SEMKEYS}

    def emit(self, engine, fn, reads=(), writes=(), tag="", dma=False,
             manual=False):
        key = semkey(engine, dma)
        inc = 16 if dma else 1
        self.counts[key] += inc
        ev = (key, self.counts[key], inc, manual, engine)
        self.items.append((engine, fn, tuple(reads), tuple(writes), tag, ev))
        return ev


def _ov(a, b):
    return a[0] == b[0] and a[1] < b[2] and b[1] < a[2]


def resolve(sched):
    last_write = {}
    readers = {}
    raw = []
    for engine, fn, reads, writes, tag, ev in sched.items:
        need = []
        for r in reads:
            for b, e in last_write.get(r[0], ()):
                if _ov(r, b):
                    need.append(e)
        for w in writes:
            for b, e in last_write.get(w[0], ()):
                if _ov(w, b):
                    need.append(e)
            for b, e in readers.get(w[0], ()):
                if _ov(w, b):
                    need.append(e)
        per = {}
        for (e_key, e_val, _inc, _m, e_eng) in need:
            per[e_key] = max(per.get(e_key, 0), e_val)
        raw.append(per)
        for w in writes:
            lw = last_write.setdefault(w[0], [])
            lw[:] = [(b, e) for b, e in lw
                     if not (w[1] <= b[1] and b[2] <= w[2])]
            lw.append((w, ev))
            rd = readers.setdefault(w[0], [])
            rd[:] = [(b, e) for b, e in rd
                     if not (w[1] <= b[1] and b[2] <= w[2])]
        for r in reads:
            readers.setdefault(r[0], []).append((r, ev))

    DRAIN_K = 8
    dkey = {e: e + "_d" for e in ("sync", "scalar", "gpsimd")}
    drains = {}
    cur = {k: 0 for k in SEMKEYS if k.endswith("_d")}
    since = {k: 0 for k in cur}
    bounds = {k: [] for k in cur}
    for idx, ((engine, fn, reads, writes, tag, ev), per) in enumerate(
            zip(sched.items, raw)):
        k = dkey.get(engine)
        if k is not None and (per or (ev[0] == k and since[k] >= DRAIN_K)):
            if since[k] > 0:
                drains.setdefault(idx, {})[k] = cur[k]
                bounds[k].append(cur[k])
                since[k] = 0
        if ev[0] in cur:
            cur[ev[0]] = ev[1]
            since[ev[0]] += 1
    for k in cur:
        if since[k] > 0:
            bounds[k].append(cur[k])
            drains.setdefault(len(sched.items), {})[k] = cur[k]

    import bisect
    def round_up(k, v):
        bl = bounds.get(k)
        if not bl:
            return v
        i = bisect.bisect_left(bl, v)
        return bl[i] if i < len(bl) else bl[-1]

    out = []
    for per in raw:
        rounded = {}
        for k, v in per.items():
            rounded[k] = round_up(k, v) if k.endswith("_d") else v
        out.append(rounded)
    return out, drains


def buf(name, lo=0, hi=1 << 30):
    return (name, lo, hi)


def build_program(detect_races=False, audit=False):
    nc = bass.Bass("TRN2", target_bir_lowering=False,
                   detect_race_conditions=detect_races)

    x_h = nc.dram_tensor("L0_attn_x_shard", [TOK, D], F32, kind="ExternalInput").ap()
    wz_h = nc.dram_tensor("L0_attn_Wz", [D, D], F32, kind="ExternalInput").ap()
    wh_h = nc.dram_tensor("L0_attn_Wh", [D, D], F32, kind="ExternalInput").ap()
    wfr_h = nc.dram_tensor("L0_ffn_w_freq_real", [D, D], F32, kind="ExternalInput").ap()
    wfi_h = nc.dram_tensor("L0_ffn_w_freq_imag", [D, D], F32, kind="ExternalInput").ap()
    brows_h = nc.dram_tensor("brows", [1, 3 * D], F32, kind="ExternalInput").ap()
    c16_h = nc.dram_tensor("C16", [D, D], F16, kind="ExternalInput").ap()
    s16_h = nc.dram_tensor("S16", [D, D], F16, kind="ExternalInput").ap()
    id16_h = nc.dram_tensor("ident16", [128, 128], F16, kind="ExternalInput").ap()
    ones16_h = nc.dram_tensor("ones16", [1, 512], F16, kind="ExternalInput").ap()
    crole_h = nc.dram_tensor("Crole", [D, 512], F16, kind="ExternalInput").ap()
    srole_h = nc.dram_tensor("Srole", [D, 512], F16, kind="ExternalInput").ap()
    midx_h = nc.dram_tensor("midx", [128, 4], I32, kind="ExternalInput").ap()
    hidx_h = nc.dram_tensor("hidx", [128, 1], I32, kind="ExternalInput").ap()
    fidx_h = nc.dram_tensor("fidx", [2, 2], I32, kind="ExternalInput").ap()
    nonce_h = nc.dram_tensor("nonce", [2, 1], I32, kind="ExternalInput").ap()
    rmask_h = nc.dram_tensor("rmask", [128, 1], F32, kind="ExternalInput").ap()
    out_h = nc.dram_tensor("out", [TOK, D], F32, kind="ExternalOutput").ap()

    sh_m = nc.dram_tensor("sh_m", [D, D], F16, addr_space="Shared").ap()
    sh_h = nc.dram_tensor("sh_h", [256, 8], F32, addr_space="Shared").ap()
    sh_flag = nc.dram_tensor("sh_flag", [16, 1], I32, addr_space="Shared").ap()

    A = {}

    def sb(name, shape, dtype):
        A[name] = nc.alloc_sbuf_tensor(name, shape, dtype).ap()
        return A[name]

    sb("sb_ident16", [128, 128], F16)
    sb("sb_brows", [1, 3 * D], F32)
    sb("brows16", [1, 3 * D], F16)
    sb("sb_ones16", [1, 512], F16)
    sb("ones_i32", [1, 8], I32)
    sb("flag_rd", [1, 8], I32)
    sb("wz16", [128, KC * D], F16)
    sb("wh16", [128, KC * D], F16)
    ar = nc.alloc_sbuf_tensor("arena", [128, 4 * 8192], F16).ap()
    WFR16 = ar[:, 0:8192]
    WFI16 = ar[:, 8192:16384]
    C16 = ar[:, 0:8192]
    S16 = ar[:, 8192:16384]
    CROLE = ar[:, 16384:20480]
    SROLE = ar[:, 20480:24576]
    M16 = ar[:, 16384:24576]
    PT2 = ar[:, 24576:28672]
    T2T2 = ar[:, 28672:32768]
    G16 = ar[:, 24576:32768]
    sb("wstage", [128, 2 * D], F32)
    sb("xpool", [128, NXS * D], F32)
    sb("xasm", [128, 2 * D], F32)
    sb("xn", [128, 2 * D], F16)
    sb("xnT", [128, 2 * 4096], F16)
    sb("pre", [128, 2 * NF * 512], F16)
    sb("sumc", [128, NT], F32)
    sb("ssqc", [128, NT], F32)
    sb("meanc", [128, NT], F32)
    sb("varc", [128, NT], F32)
    sb("stdc", [128, NT], F32)
    sb("rstdc", [128, NT], F32)
    sb("zb", [128, 512], F32)
    sb("ab", [128, 2 * 512], F32)
    sb("bb", [128, 2 * 512], F32)
    sb("hb", [128, 2 * 512], F32)
    sb("t0b", [128, 2 * 512], F16)
    sb("zeros", [128, 512], F16)
    sb("hcarry", [128, 8], F32)
    sb("sb_midx", [128, 4], I32)
    sb("sb_hidx", [128, 1], I32)
    sb("sb_fidx", [2, 2], I32)
    sb("sb_nonce", [2, 1], I32)
    sb("sb_rmask", [128, 1], F32)
    sb("flag_rd2", [1, 16], I32)
    sb("gcarry", [128, 8], F32)
    sb("hin", [128, 8], F32)
    sb("tiny", [128, 8], F32)
    sb("outst", [128, 2 * D], F32)
    sb("mtmp", [128, D], F16)

    JUNK = A["outst"][:, 0:D]

    PS = {}

    def ps(name, shape, dtype):
        PS[name] = nc.alloc_psum_tensor(name, shape, dtype).ap()
        return PS[name]

    ps("uB0", [128, 512], F32)
    ps("uB1", [128, 512], F32)
    ps("hB0", [128, 512], F32)
    ps("hB1", [128, 512], F32)
    ps("yB0", [128, 512], F32)
    ps("yB1", [128, 512], F32)
    ps("trA", [128, 1024], F16)
    ps("trB", [128, 1024], F16)

    POLLSEM = [None]
    sched = Sched()
    E = sched.emit

    def xslot(i):
        return A["xpool"][:, (i % NXS) * D:((i % NXS) + 1) * D]

    def xbuf(i):
        return buf("xpool", (i % NXS) * D, ((i % NXS) + 1) * D)

    def xaslot(i):
        return A["xasm"][:, (i % 2) * D:((i % 2) + 1) * D]

    def xabuf(i):
        return buf("xasm", (i % 2) * D, ((i % 2) + 1) * D)

    PTv = PT2
    T2v = T2T2
    T1v = A["pre"][:, 0:4096]

    def ld(dst, src, wb, tag, eng="sync"):
        E(eng, lambda e, s_, d=dst, s=src: e.dma_start(out=d, in_=s),
          writes=(wb,), tag=tag, dma=True)

    ld(A["sb_ident16"], id16_h, buf("sb_ident16"), "ld_id")
    ld(A["sb_brows"], brows_h, buf("sb_brows"), "ld_br")
    ld(A["sb_ones16"], ones16_h, buf("sb_ones16"), "ld_on")

    E("gpsimd", lambda e, s_: e.memset(A["zeros"], 0.0), writes=(buf("zeros"),), tag="mz")
    E("gpsimd", lambda e, s_: e.memset(A["hcarry"], 0.0), writes=(buf("hcarry"),), tag="mh")
    E("gpsimd", lambda e, s_: e.memset(A["gcarry"], 1.0), writes=(buf("gcarry"),), tag="mg")
    E("gpsimd", lambda e, s_: e.memset(A["ones_i32"], 1), writes=(buf("ones_i32"),), tag="mo")
    E("vector", lambda e, s_: e.tensor_copy(A["brows16"], A["sb_brows"]),
      reads=(buf("sb_brows"),), writes=(buf("brows16"),), tag="cbr")

    def cast_weight(h_ap, dst16, region, base, tag, q="sync"):
        sl = 0 if q == "sync" else 1
        stg = A["wstage"][:, sl * D:(sl + 1) * D]
        stb = buf("wstage", sl * D, (sl + 1) * D)
        for k in range(KC):
            E(q, lambda e, s_, k=k, h=h_ap, stg=stg: e.dma_start(
                out=stg, in_=h[128 * k:128 * (k + 1), :]),
              writes=(stb,), tag=f"{tag}l{k}", dma=True)
            E("vector", lambda e, s_, k=k, d=dst16, stg=stg: e.tensor_copy(
                d[:, k * D:(k + 1) * D], stg),
              reads=(stb,),
              writes=(buf(region, base + k * D, base + (k + 1) * D),),
              tag=f"{tag}c{k}")

    for k in range(KC):
        ld(CROLE[:, k * 512:(k + 1) * 512], crole_h[128 * k:128 * (k + 1), :],
           buf("arena", 16384 + k * 512, 16384 + (k + 1) * 512), f"lCr{k}",
           eng="scalar")
        ld(SROLE[:, k * 512:(k + 1) * 512], srole_h[128 * k:128 * (k + 1), :],
           buf("arena", 20480 + k * 512, 20480 + (k + 1) * 512), f"lSr{k}",
           eng="gpsimd")
    cast_weight(wfr_h, WFR16, "arena", 0, "wr", q="sync")
    cast_weight(wfi_h, WFI16, "arena", 8192, "wi", q="gpsimd")
    cast_weight(wz_h, A["wz16"], "wz16", 0, "wz", q="sync")
    cast_weight(wh_h, A["wh16"], "wh16", 0, "wh", q="gpsimd")
    for dst, src, wb, tg in ((A["sb_midx"], midx_h, buf("sb_midx"), "lmi"),
                             (A["sb_hidx"], hidx_h, buf("sb_hidx"), "lhi"),
                             (A["sb_fidx"], fidx_h, buf("sb_fidx"), "lfi"),
                             (A["sb_nonce"], nonce_h, buf("sb_nonce"), "lno"),
                             (A["sb_rmask"], rmask_h, buf("sb_rmask"), "lrm")):
        ld(dst, src, wb, tg)

    for i in range(NT):
        ld(xslot(i), x_h[128 * i:128 * (i + 1), :], xbuf(i), f"px{i}")
        E("scalar", lambda e, s_, i=i: e.activation(
            JUNK, xslot(i), AF.Identity, bias=0.0, scale=1.0,
            accum_out=A["sumc"][:, i:i + 1]),
          reads=(xbuf(i),), writes=(buf("sumc", i, i + 1), buf("outst", 0, D)),
          tag=f"sum{i}")
        E("scalar", lambda e, s_, i=i: e.activation(
            JUNK, xslot(i), AF.Square, bias=0.0, scale=1.0,
            accum_out=A["ssqc"][:, i:i + 1]),
          reads=(xbuf(i),), writes=(buf("ssqc", i, i + 1), buf("outst", 0, D)),
          tag=f"ssq{i}")

    E("vector", lambda e, s_: e.tensor_scalar(
        A["meanc"], A["sumc"], 1.0 / D, None, ALU.mult),
      reads=(buf("sumc"),), writes=(buf("meanc"),), tag="st1")
    E("vector", lambda e, s_: e.tensor_tensor(
        A["stdc"], A["meanc"], A["meanc"], ALU.mult),
      reads=(buf("meanc"),), writes=(buf("stdc"),), tag="st2")
    E("vector", lambda e, s_: e.tensor_scalar(
        A["varc"], A["ssqc"], 1.0 / D, None, ALU.mult),
      reads=(buf("ssqc"),), writes=(buf("varc"),), tag="st3")
    E("vector", lambda e, s_: e.tensor_tensor(
        A["varc"], A["varc"], A["stdc"], ALU.subtract),
      reads=(buf("varc"), buf("stdc")), writes=(buf("varc"),), tag="st4")
    E("vector", lambda e, s_: e.tensor_scalar(
        A["varc"], A["varc"], EPS, None, ALU.add),
      reads=(buf("varc"),), writes=(buf("varc"),), tag="st5")
    E("scalar", lambda e, s_: e.activation(A["stdc"], A["varc"], AF.Sqrt),
      reads=(buf("varc"),), writes=(buf("stdc"),), tag="sqrt")
    E("vector", lambda e, s_: e.reciprocal(A["rstdc"], A["stdc"]),
      reads=(buf("stdc"),), writes=(buf("rstdc"),), tag="rstd")

    slot0 = (buf("arena", 0, 8192),)
    slot1 = (buf("arena", 8192, 16384),)
    slot2 = (buf("arena", 16384, 24576),)
    slot3 = (buf("arena", 24576, 32768),)

    for oc in range(KC):
        pt_bank = PS["yB0"] if oc % 2 == 0 else PS["yB1"]
        t1_bank = PS["uB0"] if oc % 2 == 0 else PS["uB1"]
        t2_bank = PS["hB0"] if oc % 2 == 0 else PS["hB1"]
        pt_bb = buf("yB0" if oc % 2 == 0 else "yB1")
        t1_bb = buf("uB0" if oc % 2 == 0 else "uB1")
        t2_bb = buf("hB0" if oc % 2 == 0 else "hB1")

        def s1(e, s_, oc=oc, pt=pt_bank, t1=t1_bank, t2=t2_bank):
            for k in range(KC):
                e.matmul(pt, WFR16[:, k * D + 128 * oc:k * D + 128 * (oc + 1)],
                         CROLE[:, k * 512:(k + 1) * 512],
                         start=(k == 0), stop=False)
            for k in range(KC):
                e.matmul(pt, WFI16[:, k * D + 128 * oc:k * D + 128 * (oc + 1)],
                         SROLE[:, k * 512:(k + 1) * 512],
                         start=False, stop=(k == KC - 1))
            for k in range(KC):
                e.matmul(t1, WFI16[:, k * D + 128 * oc:k * D + 128 * (oc + 1)],
                         CROLE[:, k * 512:(k + 1) * 512],
                         start=(k == 0), stop=(k == KC - 1))
            last = None
            for k in range(KC):
                last = e.matmul(
                    t2, WFR16[:, k * D + 128 * oc:k * D + 128 * (oc + 1)],
                    SROLE[:, k * 512:(k + 1) * 512],
                    start=(k == 0), stop=(k == KC - 1))
            return last
        E("tensor", s1, reads=slot0 + slot1 + slot2,
          writes=(pt_bb, t1_bb, t2_bb), tag=f"ms1_{oc}")
        E("scalar", lambda e, s_, oc=oc, pt=pt_bank: e.activation(
            PTv[:, oc * 512:(oc + 1) * 512], pt, AF.Identity),
          reads=(pt_bb,),
          writes=(buf("arena", 24576 + oc * 512, 24576 + (oc + 1) * 512),),
          tag=f"ePT{oc}")
        E("scalar", lambda e, s_, oc=oc, t1=t1_bank: e.activation(
            T1v[:, oc * 512:(oc + 1) * 512], t1, AF.Identity, scale=-1.0),
          reads=(t1_bb,), writes=(buf("pre", oc * 512, (oc + 1) * 512),),
          tag=f"eT1{oc}")
        E("scalar", lambda e, s_, oc=oc, t2=t2_bank: e.activation(
            T2v[:, oc * 512:(oc + 1) * 512], t2, AF.Identity),
          reads=(t2_bb,),
          writes=(buf("arena", 28672 + oc * 512, 28672 + (oc + 1) * 512),),
          tag=f"eT2{oc}")

    for k in range(KC):
        ld(C16[:, k * D:(k + 1) * D], c16_h[128 * k:128 * (k + 1), :],
           buf("arena", k * D, (k + 1) * D), f"lC{k}", eng="scalar")
        ld(S16[:, k * D:(k + 1) * D], s16_h[128 * k:128 * (k + 1), :],
           buf("arena", 8192 + k * D, 8192 + (k + 1) * D), f"lS{k}",
           eng="scalar")

    for rc in range(4):
        bank = PS["yB0"] if rc % 2 == 0 else PS["yB1"]
        bb = buf("yB0" if rc % 2 == 0 else "yB1")
        for half in range(2):
            def s2(e, s_, rc=rc, half=half, bank=bank):
                hs = 512 * half
                last = None
                for (lv, rv) in ((PTv, C16), (T2v, S16), (T1v, S16)):
                    for oc in range(KC):
                        last = e.matmul(
                            bank,
                            lv[:, oc * 512 + 128 * rc:oc * 512 + 128 * (rc + 1)],
                            rv[:, oc * D + hs:oc * D + hs + 512],
                            start=(lv is PTv and oc == 0),
                            stop=(lv is T1v and oc == KC - 1))
                return last
            E("tensor", s2,
              reads=slot0 + slot1 + slot3 + (buf("pre", 0, 4096),),
              writes=(bb,), tag=f"ms2_{rc}_{half}")
            E("scalar", lambda e, s_, half=half, bank=bank: e.activation(
                A["mtmp"][:, half * 512:(half + 1) * 512], bank, AF.Identity,
                scale=1.0 / D),
              reads=(bb,), writes=(buf("mtmp", half * 512, (half + 1) * 512),),
              tag=f"eM{rc}_{half}")

        E("gpsimd", lambda e, s_, rc=rc: e.indirect_dma_start(
            sh_m, IndirectOffsetOnAxis(A["sb_midx"][:, rc:rc + 1], 0),
            A["mtmp"], None),
          reads=(buf("mtmp", 0, D), buf("sb_midx", rc, rc + 1)),
          writes=(buf("shm", 128 * rc, 128 * (rc + 1)),), tag=f"wM{rc}",
          dma=True)

    def m_flag_poll(e, s_):
        e.indirect_dma_start(
            sh_flag, IndirectOffsetOnAxis(A["sb_fidx"][:, 0:1], 0),
            A["sb_nonce"], None).then_inc(POLLSEM[0], 16)
        with e.register("m_rgo") as rgo, e.register("m_rn") as rn, \
             e.register("m_r0") as r0, e.register("m_r1") as r1:
            e.reg_load(rn, A["sb_nonce"][0:1, 0:1])
            e.reg_mov(rgo, 1)
            with e.While(rgo):
                for _d in range(24):
                    e.tensor_copy(A["wstage"][:, 0:128],
                                  A["zeros"][:, 0:128])
                e.dma_start(out=A["flag_rd2"],
                            in_=sh_flag).then_inc(POLLSEM[0], 16)
                e.nop()
                e.reg_load(r0, A["flag_rd2"][0:1, 0:1])
                e.reg_load(r1, A["flag_rd2"][0:1, 1:2])
                e.reg_alu(r0, r0, rn, ALU.is_equal)
                e.reg_alu(r1, r1, rn, ALU.is_equal)
                e.reg_alu(r0, r0, r1, ALU.add)
                e.reg_alu(rgo, r0, 2, ALU.is_lt)
        return e.nop()
    E("gpsimd", m_flag_poll,
      reads=(buf("shm", 0, 512), buf("sb_nonce"), buf("sb_fidx", 0, 1)),
      writes=(buf("flag_rd2"),), tag="mpoll")
    for k in range(KC):
        E("gpsimd" if k % 2 == 0 else "scalar",
          lambda e, s_, k=k: e.dma_start(
            out=M16[:, k * D:(k + 1) * D],
            in_=sh_m[128 * k:128 * (k + 1), :]),
          reads=(buf("flag_rd2"),),
          writes=(buf("arena", 16384 + k * D, 16384 + (k + 1) * D),),
          tag=f"ldM{k}", dma=True)

    def emit_tile_front(i):
        g, t = i // 4, i % 4
        gs = (g % 2) * 4096
        ld(xslot(i), x_h[128 * i:128 * (i + 1), :], xbuf(i), f"mx{i}")
        xnv = A["xn"][:, (i % 2) * D:((i % 2) + 1) * D]
        xnb = buf("xn", (i % 2) * D, ((i % 2) + 1) * D)
        E("vector", lambda e, s_, i=i, xnv=xnv: e.tensor_scalar(
            xnv, xslot(i), A["meanc"][:, i:i + 1], A["rstdc"][:, i:i + 1],
            ALU.subtract, ALU.mult),
          reads=(xbuf(i), buf("meanc", i, i + 1), buf("rstdc", i, i + 1)),
          writes=(xnb,), tag=f"xn{i}")

        def trx(e, s_, xnv=xnv):
            last = None
            for k in range(KC):
                last = e.transpose(PS["trA"][:, 128 * k:128 * (k + 1)],
                                   xnv[:, 128 * k:128 * (k + 1)], A["sb_ident16"])
            return last
        E("tensor", trx, reads=(xnb, buf("sb_ident16")),
          writes=(buf("trA"),), tag=f"trx{i}")

        def exn(e, s_, gs=gs, t=t):
            out3 = A["xnT"][:, gs:gs + 4096].rearrange(
                "p (k q) -> p k q", k=KC)[:, :, t * 128:(t + 1) * 128]
            in3 = PS["trA"].rearrange("p (k q) -> p k q", k=KC)
            return e.copy(out3, in3)
        E("scalar", exn, reads=(buf("trA"),),
          writes=tuple(buf("xnT", gs + k * 512 + t * 128,
                           gs + k * 512 + (t + 1) * 128) for k in range(KC)),
          tag=f"exn{i}")

    def emit_group_f(g, f):
        gs = (g % 2) * 4096
        xnT_read = buf("xnT", gs, gs + 4096)
        ub = PS["uB0"] if f % 2 == 0 else PS["uB1"]
        hbk = PS["hB0"] if f % 2 == 0 else PS["hB1"]
        yb = PS["yB0"] if f % 2 == 0 else PS["yB1"]
        ubb = buf("uB0" if f % 2 == 0 else "uB1")
        hbb = buf("hB0" if f % 2 == 0 else "hB1")
        ybb = buf("yB0" if f % 2 == 0 else "yB1")
        fs = (f % 2) * 512

        def mmproj(e, w16, bias_off, bank, gs=gs, f=f):
            e.matmul(bank,
                     A["brows16"][:, bias_off + f * 128:bias_off + (f + 1) * 128],
                     A["sb_ones16"], start=True, stop=False)
            last = None
            for k in range(KC):
                last = e.matmul(
                    bank, w16[:, k * D + 128 * f:k * D + 128 * (f + 1)],
                    A["xnT"][:, gs + k * 512:gs + (k + 1) * 512],
                    start=False, stop=(k == KC - 1))
            return last

        E("tensor", lambda e, s_: mmproj(e, A["wz16"], 0, ub),
          reads=(xnT_read, buf("wz16"), buf("brows16"), buf("sb_ones16")),
          writes=(ubb,), tag=f"u{g}_{f}")
        E("scalar", lambda e, s_: e.activation(A["zb"], ub, AF.Sigmoid),
          reads=(ubb,), writes=(buf("zb"),), tag=f"z{g}_{f}")
        E("scalar", lambda e, s_: e.activation(
            A["ab"][:, fs:fs + 512], ub, AF.Sigmoid, scale=-1.0),
          reads=(ubb,), writes=(buf("ab", fs, fs + 512),), tag=f"a{g}_{f}")
        E("tensor", lambda e, s_: mmproj(e, A["wh16"], D, hbk),
          reads=(xnT_read, buf("wh16"), buf("brows16"), buf("sb_ones16")),
          writes=(hbb,), tag=f"h{g}_{f}")
        E("vector", lambda e, s_: e.tensor_tensor(
            A["bb"][:, fs:fs + 512], A["zb"], hbk, ALU.mult),
          reads=(buf("zb"), hbb), writes=(buf("bb", fs, fs + 512),),
          tag=f"b{g}_{f}")

        E("vector", lambda e, s_, f=f, fs=fs: e.tensor_tensor_scan(
            A["hb"][:, fs:fs + 512], A["ab"][:, fs:fs + 512],
            A["bb"][:, fs:fs + 512], A["hcarry"][:, f:f + 1],
            ALU.mult, ALU.add),
          reads=(buf("ab", fs, fs + 512), buf("bb", fs, fs + 512),
                 buf("hcarry", f, f + 1)),
          writes=(buf("hb", fs, fs + 512),), tag=f"sh{g}_{f}")
        E("vector", lambda e, s_, f=f, fs=fs: e.tensor_copy(
            A["hcarry"][:, f:f + 1], A["hb"][:, fs + 511:fs + 512]),
          reads=(buf("hb", fs, fs + 512),),
          writes=(buf("hcarry", f, f + 1),), tag=f"shc{g}_{f}")

        if g < 2:
            goff = (g * 8 + f) * 512

            E("vector", lambda e, s_, f=f, fs=fs, goff=goff: e.tensor_tensor_scan(
                G16[:, goff:goff + 512], A["ab"][:, fs:fs + 512],
                A["zeros"], A["gcarry"][:, f:f + 1], ALU.mult, ALU.add),
              reads=(buf("ab", fs, fs + 512), buf("zeros"),
                     buf("gcarry", f, f + 1)),
              writes=(buf("arena", 24576 + goff, 24576 + goff + 512),),
              tag=f"sg{g}_{f}")
            E("vector", lambda e, s_, f=f, goff=goff: e.tensor_copy(
                A["gcarry"][:, f:f + 1], G16[:, goff + 511:goff + 512]),
              reads=(buf("arena", 24576 + goff, 24576 + goff + 512),),
              writes=(buf("gcarry", f, f + 1),), tag=f"sgc{g}_{f}")

        E("tensor", lambda e, s_: mmproj(e, M16, 2 * D, yb),
          reads=(xnT_read, buf("arena", 16384, 24576), buf("brows16"),
                 buf("sb_ones16")),
          writes=(ybb,), tag=f"y{g}_{f}")
        E("scalar", lambda e, s_: e.activation(
            A["t0b"][:, fs:fs + 512], yb, AF.Identity),
          reads=(ybb,), writes=(buf("t0b", fs, fs + 512),), tag=f"t0{g}_{f}")
        poff = (g % 2) * 4096 + f * 512
        E("gpsimd", lambda e, s_, poff=poff, fs=fs: e.tensor_tensor(
            A["pre"][:, poff:poff + 512], A["t0b"][:, fs:fs + 512],
            A["hb"][:, fs:fs + 512], ALU.add),
          reads=(buf("t0b", fs, fs + 512), buf("hb", fs, fs + 512)),
          writes=(buf("pre", poff, poff + 512),), tag=f"pre{g}_{f}")

    def emit_tile_back(i):
        g, t = i // 4, i % 4
        poffg = (g % 2) * 4096
        ld(xaslot(i), x_h[128 * i:128 * (i + 1), :], xabuf(i), f"ax{i}")

        def trp(e, s_, t=t, poffg=poffg):
            last = None
            for f in range(NF):
                last = e.transpose(
                    PS["trB"][:, 128 * f:128 * (f + 1)],
                    A["pre"][:, poffg + f * 512 + t * 128:
                             poffg + f * 512 + (t + 1) * 128],
                    A["sb_ident16"])
            return last
        E("tensor", trp,
          reads=(buf("pre", poffg, poffg + 4096), buf("sb_ident16")),
          writes=(buf("trB"),), tag=f"trp{i}")
        osl = (i % 2) * D
        E("vector", lambda e, s_, i=i, osl=osl: e.tensor_tensor(
            A["outst"][:, osl:osl + D], xaslot(i), PS["trB"], ALU.add),
          reads=(xabuf(i), buf("trB")),
          writes=(buf("outst", osl, osl + D),), tag=f"oadd{i}")
        E("scalar", lambda e, s_, i=i, osl=osl: e.dma_start(
            out=out_h[128 * i:128 * (i + 1), :],
            in_=A["outst"][:, osl:osl + D]),
          reads=(buf("outst", osl, osl + D),),
          writes=(buf("outhbm", 128 * i, 128 * (i + 1)),), tag=f"st{i}",
          dma=True)

    for g in range(NG):
        for t in range(4):
            emit_tile_front(4 * g + t)
        for f in range(NF):
            emit_group_f(g, f)
        if g >= 1:
            for t in range(4):
                emit_tile_back(4 * (g - 1) + t)
    for t in range(4):
        emit_tile_back(4 * 7 + t)

    E("gpsimd", lambda e, s_: e.indirect_dma_start(
        sh_h, IndirectOffsetOnAxis(A["sb_hidx"], 0), A["hcarry"], None),
      reads=(buf("hcarry", 0, 8), buf("sb_hidx")),
      writes=(buf("shh"),), tag="hxw", dma=True)

    def h_flag_poll(e, s_):
        e.indirect_dma_start(
            sh_flag, IndirectOffsetOnAxis(A["sb_fidx"][:, 1:2], 0),
            A["sb_nonce"], None).then_inc(POLLSEM[0], 16)
        with e.register("h_rgo") as rgo, e.register("h_rn") as rn, \
             e.register("h_r0") as r0, e.register("h_r1") as r1:
            e.reg_load(rn, A["sb_nonce"][0:1, 0:1])
            e.reg_mov(rgo, 1)
            with e.While(rgo):
                for _d in range(24):
                    e.tensor_copy(A["wstage"][:, 0:128],
                                  A["zeros"][:, 0:128])
                e.dma_start(out=A["flag_rd2"],
                            in_=sh_flag).then_inc(POLLSEM[0], 16)
                e.nop()
                e.reg_load(r0, A["flag_rd2"][0:1, 2:3])
                e.reg_load(r1, A["flag_rd2"][0:1, 3:4])
                e.reg_alu(r0, r0, rn, ALU.is_equal)
                e.reg_alu(r1, r1, rn, ALU.is_equal)
                e.reg_alu(r0, r0, r1, ALU.add)
                e.reg_alu(rgo, r0, 2, ALU.is_lt)
        return e.nop()
    E("gpsimd", h_flag_poll,
      reads=(buf("shh"), buf("sb_nonce"), buf("sb_fidx", 1, 2)),
      writes=(buf("flag_rd2"),), tag="hpoll")
    E("gpsimd", lambda e, s_: e.dma_start(out=A["tiny"], in_=sh_h[0:128, :]),
      reads=(buf("flag_rd2"),), writes=(buf("tiny"),), tag="hrd", dma=True)
    E("vector", lambda e, s_: e.tensor_scalar_mul(
        A["hin"], A["tiny"], A["sb_rmask"]),
      reads=(buf("tiny"), buf("sb_rmask")), writes=(buf("hin"),), tag="hmask")

    CORR = A["mtmp"]
    for i in range(8):
        g, t = i // 4, i % 4
        E("gpsimd", lambda e, s_, i=i: e.dma_start(
            out=xaslot(i), in_=out_h[128 * i:128 * (i + 1), :]),
          reads=(buf("outhbm", 128 * i, 128 * (i + 1)),),
          writes=(xabuf(i),), tag=f"rb{i}", dma=True)

        def mkcorr(e, s_, g=g, t=t):
            last = None
            for f in range(NF):
                goff = (g * 8 + f) * 512 + t * 128
                last = e.tensor_scalar_mul(
                    CORR[:, f * 128:(f + 1) * 128],
                    G16[:, goff:goff + 128], A["hin"][:, f:f + 1])
            return last
        E("vector", mkcorr, reads=(buf("arena", 24576, 32768), buf("hin")),
          writes=(buf("mtmp", 0, D),), tag=f"corr{i}")

        def trc(e, s_):
            last = None
            for f in range(NF):
                last = e.transpose(PS["trB"][:, 128 * f:128 * (f + 1)],
                                   CORR[:, f * 128:(f + 1) * 128],
                                   A["sb_ident16"])
            return last
        E("tensor", trc, reads=(buf("mtmp", 0, D), buf("sb_ident16")),
          writes=(buf("trB"),), tag=f"trc{i}")
        osl = (i % 2) * D
        E("vector", lambda e, s_, i=i, osl=osl: e.tensor_tensor(
            A["outst"][:, osl:osl + D], xaslot(i), PS["trB"], ALU.add),
          reads=(xabuf(i), buf("trB")),
          writes=(buf("outst", osl, osl + D),), tag=f"o3_{i}")
        E("scalar", lambda e, s_, i=i, osl=osl: e.dma_start(
            out=out_h[128 * i:128 * (i + 1), :],
            in_=A["outst"][:, osl:osl + D]),
          reads=(buf("outst", osl, osl + D),),
          writes=(buf("outhbm", 128 * i, 128 * (i + 1)),), tag=f"s3_{i}",
          dma=True)

    waits, drains = resolve(sched)
    per_engine = {e: [] for e in ENGS}
    per_engine_idx = {e: [] for e in ENGS}
    for gi, (item, w) in enumerate(zip(sched.items, waits)):
        per_engine[item[0]].append((item, w))
        per_engine_idx[item[0]].append(gi)

    from contextlib import ExitStack
    with ExitStack() as stack:
        sems = {k: stack.enter_context(nc.semaphore(f"sem_{k}")) for k in SEMKEYS}
        sems["poll"] = stack.enter_context(nc.semaphore("sem_poll"))
        POLLSEM[0] = sems["poll"]
        block = stack.enter_context(nc.Block())

        def emit_stream(eng_name, eng):
            hw = {k: 0 for k in SEMKEYS}
            for gi, ((engine, fn, reads, writes, tag, ev), w) in zip(
                    per_engine_idx[eng_name], per_engine[eng_name]):
                dr = drains.get(gi, {})
                for dkey2, dval in sorted(dr.items()):
                    if dval > hw[dkey2]:
                        eng.wait_ge(sems[dkey2], dval)
                        hw[dkey2] = dval
                for wkey, wval in sorted(w.items()):
                    if wval > hw[wkey]:
                        eng.wait_ge(sems[wkey], wval)
                        hw[wkey] = wval
                last = fn(eng, sems[ev[0]])
                if not ev[3]:
                    assert last is not None, f"item {tag} returned None"
                    last.then_inc(sems[ev[0]], ev[2])
            if eng_name == "gpsimd":
                eng.wait_ge(sems["scalar_c"], sched.counts["scalar_c"])
                eng.wait_ge(sems["scalar_d"], sched.counts["scalar_d"])
                eng.wait_ge(sems["sync_d"], sched.counts["sync_d"])

        @block.sync
        def _(sync):
            emit_stream("sync", sync)

        @block.tensor
        def _(tensor):
            emit_stream("tensor", tensor)

        @block.vector
        def _(vector):
            emit_stream("vector", vector)

        @block.scalar
        def _(scalar):
            emit_stream("scalar", scalar)

        @block.gpsimd
        def _(gpsimd):
            emit_stream("gpsimd", gpsimd)

    return nc


_NC_CACHE = []


def kernel(L0_attn_x, L0_attn_Wz, L0_attn_bz, L0_attn_Wh, L0_attn_bh,
           L0_ffn_w_freq_real, L0_ffn_w_freq_imag, L0_ffn_bias):
    x = np.ascontiguousarray(np.asarray(L0_attn_x, np.float32)).reshape(4 * 8192, D)
    wz = np.ascontiguousarray(np.asarray(L0_attn_Wz, np.float32))
    wh = np.ascontiguousarray(np.asarray(L0_attn_Wh, np.float32))
    wfr = np.ascontiguousarray(np.asarray(L0_ffn_w_freq_real, np.float32))
    wfi = np.ascontiguousarray(np.asarray(L0_ffn_w_freq_imag, np.float32))
    brows = np.ascontiguousarray(np.concatenate(
        [np.asarray(L0_attn_bz, np.float32),
         np.asarray(L0_attn_bh, np.float32),
         np.asarray(L0_ffn_bias, np.float32)]).reshape(1, 3 * D))

    k = np.arange(D, dtype=np.float64)
    ang = 2.0 * np.pi * np.outer(k, k) / D
    C16 = np.cos(ang).astype(np.float16)
    S16 = np.sin(ang).astype(np.float16)
    id16 = np.eye(128, dtype=np.float16)
    ones16 = np.ones((1, 512), np.float16)

    if not _NC_CACHE:
        _NC_CACHE.append(build_program())
    nc = _NC_CACHE[0]

    nonce = np.full((2, 1), np.random.randint(1, 2 ** 30), np.int32)
    in_maps = []
    for c in range(NCORES):
        role = c % 2
        in_maps.append({
            "L0_attn_x_shard": np.ascontiguousarray(x[TOK * c:TOK * (c + 1)]),
            "L0_attn_Wz": wz, "L0_attn_Wh": wh,
            "L0_ffn_w_freq_real": wfr, "L0_ffn_w_freq_imag": wfi,
            "brows": brows, "C16": C16, "S16": S16,
            "Crole": np.ascontiguousarray(C16[:, 512 * role:512 * (role + 1)]),
            "Srole": np.ascontiguousarray(S16[:, 512 * role:512 * (role + 1)]),
            "midx": np.ascontiguousarray(
                (np.arange(128, dtype=np.int32)[:, None]
                 + 512 * role + 128 * np.arange(4, dtype=np.int32)[None, :])),
            "hidx": (np.arange(128, dtype=np.int32) + 128 * role
                     ).reshape(128, 1),
            "fidx": np.array([[role, 2 + role], [8 + role, 8 + role]],
                             np.int32),
            "nonce": nonce,
            "rmask": np.full((128, 1), float(role), np.float32),
            "ident16": id16, "ones16": ones16,
        })
    res = run_bass_kernel_spmd(nc, in_maps, core_ids=list(range(NCORES)))
    out = np.concatenate([res.results[c]["out"] for c in range(NCORES)], axis=0)
    return out.reshape(4, 8192, D)
